# version 11
# baseline (speedup 1.0000x reference)
import numpy as np
import ml_dtypes
from contextlib import ExitStack

import concourse.bass as bass
import concourse.mybir as mybir
from concourse.bass_utils import run_bass_kernel_spmd
from concourse.library_config import (sparse_gather as LIB_SG, mlp as LIB_MLP,
                                      standard as LIB_STD)
from concourse.library_overlay import lower_extended_insts

F32 = mybir.dt.float32
BF16 = mybir.dt.bfloat16
I16 = mybir.dt.int16
U32 = mybir.dt.uint32
AF = mybir.ActivationFunctionType
OP = mybir.AluOpType
BF = ml_dtypes.bfloat16

NCORES = 8
B, T, D = 16, 2048, 1024
NTOK_G = B * T
NTOK = NTOK_G // NCORES
DC = D // 128
CH = 512
NB = CH // 128
NCHUNK = NTOK // CH
NE = 8
FF = 4096
FC = FF // 128
CAP = 1152
NHEAD = 16
EPS_LN = 1e-5
EPS_ATT = 1e-6
NSEARCH = 48
SBOUND = 512.0
TWO23 = float(2 ** 23)
PI = float(np.pi)


class Ctr:
    def __init__(self, stack, nc, name):
        self.sem = stack.enter_context(nc.semaphore(name))
        self.n = 0

    def inc(self, ins, k=1):
        ins.then_inc(self.sem, k)
        self.n += k
        return self.n

    def wait(self, eng, n=None):
        eng.wait_ge(self.sem, self.n if n is None else n)


def _consts(nc):
    c = {}

    def add(name, arr):
        c[name] = nc.inline_tensor(np.ascontiguousarray(arr), name=name).ap()

    i128 = np.arange(128, dtype=np.float32)
    add("iota512_f", np.arange(512, dtype=np.float32).reshape(1, 512))
    add("ones_row_f", np.ones((1, 128), np.float32))
    add("ones_row_b", np.ones((1, 128), BF))
    add("ones512_f", np.ones((1, 512), np.float32))
    add("ones_col_b", np.ones((128, 1), BF))
    add("ones_col_f", np.ones((128, 1), np.float32))
    add("ident_f", np.eye(128, dtype=np.float32))
    add("ident_b", np.eye(128, dtype=BF))
    for q in range(NB):
        add(f"upat{q}_b", ((128 * q + i128[:, None]) <= np.arange(512)[None, :]).astype(BF))
    er = np.zeros((8, DC * 512), np.float32)
    for cidx in range(DC):
        er[cidx, cidx * 512:(cidx + 1) * 512] = 1.0
    add("erow_b", er.astype(BF))
    add("caus_b", (i128[:, None] <= i128[None, :]).astype(BF))
    add("iota16_f", (np.arange(16)[:, None] * 256 + np.arange(256)[None, :]).astype(np.float32))
    add("iota2_f", (np.arange(16)[:, None] + 16 * np.arange(256)[None, :]).astype(np.float32))
    i16 = np.arange(16)
    add("tri16s_f", (i16[:, None] < i16[None, :]).astype(np.float32))
    add("tri128s_f", (i128[:, None] < i128[None, :]).astype(np.float32))
    return c


class KB:
    def __init__(self):
        self.nc = bass.Bass("TRN2", target_bir_lowering=False)
        nc = self.nc
        self.CST = _consts(nc)
        self.S = ExitStack()
        self.row0 = 0

        din = {}

        def DI(name, shape):
            din[name] = nc.dram_tensor(name, list(shape), F32, kind="ExternalInput").ap()

        DI("x_full", (NTOK_G, D))
        DI("Win", (3, D, D)); DI("bin_", (3, D)); DI("plog", (3, D))
        DI("Wmid", (2, D, D)); DI("bmid", (2, D))
        DI("w1w", (D, D)); DI("w1b", (1, D)); DI("w2w", (D, D)); DI("w2b", (1, D))
        DI("lng", (1, D)); DI("lnb", (1, D))
        DI("Wg", (D, NE)); DI("W1", (NE, D, FF)); DI("b1", (NE, FF))
        DI("W2", (NE, FF, D)); DI("b2", (NE, D))
        DI("wattn", (D, 3 * D)); DI("wproj", (D, D)); DI("L1w", (D, D))
        self.din = din

        self.out_hbm = nc.dram_tensor("out", [NTOK, D], F32, kind="ExternalOutput").ap()
        self.xbf = nc.dram_tensor("xbf", [NTOK_G, D], BF16, kind="Internal").ap()
        self.xn_hbm = nc.dram_tensor("xn_hbm", [NTOK, D], BF16, kind="Internal").ap()
        self.x1_hbm = nc.dram_tensor("x1_hbm", [NTOK, D], F32, kind="Internal").ap()
        self.sc_hbm = nc.dram_tensor("sc_hbm", [16, 10, 2048], F32, kind="Internal").ap()
        self.ho_hbm = nc.dram_tensor("ho_hbm", [3, DC, 128, NTOK], BF16, kind="Internal").ap()
        self.xn2_hbm = nc.dram_tensor("xn2_hbm", [NTOK, D], BF16, kind="Internal").ap()
        self.yat_hbm = nc.dram_tensor("yat_hbm", [NTOK, D], BF16, kind="Internal").ap()
        self.moebuf = nc.dram_tensor("moebuf", [NE, CAP + 1, D], BF16,
                                     kind="Internal").ap()
        self.w1bf = nc.dram_tensor("w1bf", [NE, D, FF], BF16, kind="Internal").ap()
        self.w2bf = nc.dram_tensor("w2bf", [NE, FF, D], BF16, kind="Internal").ap()

        e = self.S.enter_context
        self.onesr = e(nc.sbuf_tensor("onesr", [1, 128], F32))
        self.onesrb = e(nc.sbuf_tensor("onesrb", [1, 128], BF16))
        self.ones512 = e(nc.sbuf_tensor("ones512", [1, 512], F32))
        self.onesc = e(nc.sbuf_tensor("onesc", [128, 1], BF16))
        self.onescf = e(nc.sbuf_tensor("onescf", [128, 1], F32))
        self.ident = e(nc.sbuf_tensor("ident", [128, 128], F32))
        self.identb = e(nc.sbuf_tensor("identb", [128, 128], BF16))
        self.cw = e(nc.sbuf_tensor("cw", [1, 16], F32))
        self.wg9 = e(nc.sbuf_tensor("wg9", [128, DC, 9], BF16))
        self.cnt8 = e(nc.sbuf_tensor("cnt8", [128, 8], F32))
        self.lo8 = e(nc.sbuf_tensor("lo8", [1, 8], F32))
        self.hi8 = e(nc.sbuf_tensor("hi8", [1, 8], F32))
        self.mid8 = e(nc.sbuf_tensor("mid8", [1, 8], F32))
        self.ge8 = e(nc.sbuf_tensor("ge8", [1, 8], F32))
        self.t8 = e(nc.sbuf_tensor("t8", [1, 8], F32))
        self.u8 = e(nc.sbuf_tensor("u8", [1, 8], F32))
        self.th = e(nc.sbuf_tensor("th", [128, 8], F32))
        self.cwbc = e(nc.sbuf_tensor("cwbc", [128, 8], F32))
        self.lists = e(nc.sbuf_tensor("lists_", [128, NE, 256], I16))
        self.nf = e(nc.sbuf_tensor("nf_", [1, NE], U32))
        self.nf2 = e(nc.sbuf_tensor("nf2_", [1, NE], U32))
        self.gv9 = e(nc.sbuf_tensor("gv9", [128, NE, 9], F32))
        self.inv16 = e(nc.sbuf_tensor("inv16", [128, NE, 256], I16))
        self.mv = e(nc.sbuf_tensor("mv", [128, 37, 16], F32))
        self.epsc = e(nc.sbuf_tensor("epsc", [128, 1], F32))
        self.s1 = e(nc.sbuf_tensor("s1", [128, 2048], F32))
        self.s2 = e(nc.sbuf_tensor("s2", [128, 2048], F32))

        self.ld = Ctr(self.S, nc, "c_ld")
        self.lda = Ctr(self.S, nc, "c_lda")
        self.lsw = Ctr(self.S, nc, "c_lsw")
        self.lxb = Ctr(self.S, nc, "c_lxb")
        self.pe = Ctr(self.S, nc, "c_pe")
        self.ve = Ctr(self.S, nc, "c_ve")
        self.ac = Ctr(self.S, nc, "c_ac")
        self.gp = Ctr(self.S, nc, "c_gp")
        self.sgd = Ctr(self.S, nc, "c_sgd")
        self.wcast = Ctr(self.S, nc, "c_wcast")
        self.wl = [Ctr(self.S, nc, f"c_wl{i}") for i in range(4)]
        self.st = [Ctr(self.S, nc, f"c_st{i}") for i in range(4)]
        self.stx = [Ctr(self.S, nc, "c_stx0"), Ctr(self.S, nc, "c_stx1")]
        self.trl = [Ctr(self.S, nc, f"c_trl{i}") for i in range(2)]

    def vs(self, v, ins):
        self.ve.inc(ins)
        self.ve.wait(v)

    def bar(self):
        self.nc.all_engine_barrier()

    def p0_loads(self):
        nc, din, CST = self.nc, self.din, self.CST
        ld, lsw, ve, ac = self.ld, self.lsw, self.ve, self.ac
        with nc.Block() as blk:
            @blk.sync
            def _(sy):
                for name, sb in [("ones_row_f", self.onesr), ("ones_row_b", self.onesrb),
                                 ("ones512_f", self.ones512),
                                 ("ones_col_b", self.onesc), ("ones_col_f", self.onescf),
                                 ("ident_f", self.ident), ("ident_b", self.identb)]:
                    ld.inc(sy.dma_start(sb[:], CST[name]), 16)

            @blk.gpsimd
            def _(g):
                for k in range(DC):
                    lsw.inc(g.dma_start(self.wg9[:, k, 0:8],
                                        din["Wg"][k * 128:(k + 1) * 128, :]), 16)
                for i in range(8):
                    sl = slice(i * (NTOK_G // 8), (i + 1) * (NTOK_G // 8))
                    self.lxb.inc(g.dma_start(self.xbf[sl, :], din["x_full"][sl, :]), 16)
                for e_ in range(NE):
                    self.wcast.inc(g.dma_start(self.w1bf[e_], din["W1"][e_]), 16)
                    self.wcast.inc(g.dma_start(self.w2bf[e_], din["W2"][e_]), 16)

            @blk.vector
            def _(v):
                ld.wait(v)
                v.wait_ge(lsw.sem, 16 * DC)
                for k in range(DC):
                    ve.inc(v.memset(self.wg9[:, k, 8:9], 1.0))
                ve.inc(v.memset(self.epsc[:], float(EPS_LN)))
                v.wait_ge(ve.sem, ve.n)
        self.bar()

    def p_ln(self, src_is_x, dst_hbm):
        nc = self.nc
        tag = "a" if src_is_x else "b"
        ld, ve, ac, st = self.ld, self.ve, self.ac, self.st
        R = ExitStack()
        r = R.enter_context
        xtm = r(nc.sbuf_tensor(f"ln_xtm_{tag}", [128, 32, D], F32))
        xnb = r(nc.sbuf_tensor(f"ln_xnb_{tag}", [128, 4, D], BF16))
        with nc.Block() as blk:
            @blk.sync
            def _(sy):
                for i in range(32):
                    if src_is_x:
                        src = self.din["x_full"][self.row0 + i * 128: self.row0 + (i + 1) * 128, :]
                    else:
                        src = self.x1_hbm[i * 128:(i + 1) * 128, :]
                    ld.inc(sy.dma_start(xtm[:, i, :], src), 16)

            @blk.vector
            def _(v):
                ld.wait(v)
                for i in range(32):
                    self.vs(v, v.bn_stats(self.mv[:, i, 0:6], xtm[:, i, 0:512]))
                    self.vs(v, v.bn_stats(self.mv[:, i, 6:12], xtm[:, i, 512:1024]))
                    self.vs(v, v.bn_aggr(self.mv[:, i, 12:14], self.mv[:, i, 0:12]))

            @blk.scalar
            def _(s):
                ve.wait(s)
                ac.inc(s.activation(self.mv[:, 0:32, 14:15], self.mv[:, 0:32, 13:14],
                                    AF.Sqrt, bias=self.epsc[:, 0:1]))
        st_base = [c.n for c in st]
        with nc.Block() as blk:
            ve_marks = []

            @blk.vector
            def _(v):
                ac.wait(v)
                self.vs(v, v.reciprocal(self.mv[:, 0:32, 15:16], self.mv[:, 0:32, 14:15]))
                for i in range(32):
                    if i >= 4:
                        st[i % 4].wait(v, st_base[i % 4] + 16 * (i // 4))
                    self.vs(v, v.tensor_scalar(xnb[:, i % 4, :], xtm[:, i, :],
                                               self.mv[:, i, 12:13], self.mv[:, i, 15:16],
                                               OP.subtract, OP.mult))
                    ve_marks.append(self.ve.n)

            @blk.sync
            def _(sy):
                for i in range(32):
                    sy.wait_ge(ve.sem, ve_marks[i])
                    st[i % 4].inc(sy.dma_start(dst_hbm[i * 128:(i + 1) * 128, :],
                                               xnb[:, i % 4, :]), 16)
        self.bar()
        with nc.Block() as blk:
            @blk.sync
            def _(sy):
                for j in range(4):
                    st[j].wait(sy)
        self.bar()
        R.close()

    def p2_scores(self):
        nc, din = self.nc, self.din
        ld, lsw, pe, ve, ac, gp, st, trl = (self.ld, self.lsw, self.pe, self.ve,
                                            self.ac, self.gp, self.st, self.trl)
        sgd = self.sgd
        R = ExitStack()
        r = R.enter_context
        xfm = r(nc.sbuf_tensor("p2_xfm", [128, 2, 2048], BF16))
        x2f = r(nc.sbuf_tensor("p2_x2f", [128, 2, 2048], BF16))
        sc10 = r(nc.sbuf_tensor("p2_sc10", [9, 2048], F32))
        ssq = r(nc.sbuf_tensor("p2_ssq", [1, 2048], F32))
        self.scores = r(nc.sbuf_tensor("scores", [128, NE * 256], F32))
        self.mu = r(nc.sbuf_tensor("mu_", [128, 256], F32))
        self.sq = r(nc.sbuf_tensor("sq_", [128, 256], F32))
        self.rstd = r(nc.sbuf_tensor("rstd_", [128, 256], F32))
        self.srch = r(nc.sbuf_tensor("srch", [128, NE * 256], BF16))
        self.gmy = r(nc.sbuf_tensor("gmy", [16, 256], F32))
        self.mif = r(nc.sbuf_tensor("mif", [16, 256], F32))
        self.m01 = r(nc.sbuf_tensor("m01", [16, 256], F32))
        self.mii = r(nc.sbuf_tensor("mii", [16, 256], I16))
        self.iota16 = r(nc.sbuf_tensor("iota16", [16, 256], F32))
        glist = r(nc.sbuf_tensor("p2_glist", [16, 256], F32))
        pb = [r(nc.psum_tensor(f"p2_pb{i}", [128, 512], F32)) for i in range(8)]

        with nc.Block() as blk:
            @blk.sync
            def _(sy):
                ld.inc(sy.dma_start(self.iota16[:], self.CST["iota2_f"]), 16)

            @blk.tensor
            def _(t):
                self.lsw.wait(t, 16 * DC)
                ve.wait(t)
                for k in range(DC):
                    pe.inc(t.matmul(pb[0][0:1, 0:9], self.onesc[:], self.wg9[:, k, :],
                                    start=(k == 0), stop=(k == DC - 1)))

            @blk.vector
            def _(v):
                pe.wait(v)
                self.vs(v, v.tensor_copy(self.cw[:, 0:9], pb[0][0:1, 0:9]))
        self.bar()

        for i in range(16):
            for k in range(DC):
                par = k % 2
                with nc.Block() as blk:
                    @blk.sync
                    def _(sy, i=i, k=k, par=par):
                        pe.wait(sy)
                        self.lxb.wait(sy)
                        trl[par].inc(sy.dma_start(
                            xfm[:, par, :],
                            self.xbf[i * 2048:(i + 1) * 2048, k * 128:(k + 1) * 128],
                            transpose=True), 16)

                    @blk.scalar
                    def _(s, i=i, k=k, par=par):
                        trl[par].wait(s)
                        ac.inc(s.activation(x2f[:, par, :], xfm[:, par, :], AF.Square))

                    @blk.tensor
                    def _(t, i=i, k=k, par=par):
                        ac.wait(t)
                        trl[par].wait(t)
                        if k == 0:
                            ve.wait(t)
                        for n in range(4):
                            sl = slice(n * 512, (n + 1) * 512)
                            t.matmul(pb[n][0:9, :], self.wg9[:, k, :],
                                     xfm[:, par, sl], start=(k == 0), stop=(k == DC - 1))
                            pe.inc(t.matmul(pb[4 + n][0:1, :], self.onesc[:],
                                            x2f[:, par, sl], start=(k == 0),
                                            stop=(k == DC - 1)))
            with nc.Block() as blk:
                @blk.vector
                def _(v, i=i):
                    pe.wait(v)
                    if i >= 1:
                        st[2].wait(v)
                    for n in range(4):
                        sl = slice(n * 512, (n + 1) * 512)
                        self.vs(v, v.tensor_copy(sc10[:, sl], pb[n][0:9, :]))
                        self.vs(v, v.tensor_copy(ssq[:, sl], pb[4 + n][0:1, :]))

                @blk.sync
                def _(sy, i=i):
                    ve.wait(sy)
                    st[2].inc(sy.dma_start(self.sc_hbm[i, 0:9, :], sc10[:]), 16)
                    st[2].inc(sy.dma_start(self.sc_hbm[i, 9:10, :], ssq[:]), 16)
        self.bar()
        with nc.Block() as blk:
            @blk.sync
            def _(sy):
                st[2].wait(sy)
        self.bar()

        with nc.Block() as blk:
            @blk.sync
            def _(sy):
                sc4 = self.sc_hbm.rearrange("i r (jb f) -> i r jb f", jb=8)
                for a in range(16):
                    ld.inc(sy.dma_start(
                        self.scores[8 * a:8 * a + 8, :].rearrange("p (e f) -> p e f", e=8),
                        sc4[a, 0:8, :, :].rearrange("e b f -> b e f")), 16)
                    ld.inc(sy.dma_start(self.mu[8 * a:8 * a + 8, :], sc4[a, 8, :, :]), 16)
                    ld.inc(sy.dma_start(self.sq[8 * a:8 * a + 8, :], sc4[a, 9, :, :]), 16)

            @blk.vector
            def _(v):
                ld.wait(v)
                self.vs(v, v.tensor_scalar(self.mu[:], self.mu[:], 1.0 / D, None, OP.mult))
                self.vs(v, v.tensor_tensor(self.rstd[:], self.mu[:], self.mu[:], OP.mult))
                self.vs(v, v.scalar_tensor_tensor(self.rstd[:], self.sq[:], 1.0 / D,
                                                  self.rstd[:], OP.mult, OP.subtract))

            @blk.scalar
            def _(s):
                ve.wait(s)
                ac.inc(s.activation(self.rstd[:], self.rstd[:], AF.Sqrt, bias=self.epsc[:, 0:1]))

            @blk.vector
            def _(v):
                ac.wait(v)
                self.vs(v, v.reciprocal(self.rstd[:], self.rstd[:]))

            @blk.tensor
            def _(t):
                ve.wait(t)
                pe.inc(t.matmul(pb[0][0:128, 0:8], self.onesr[:], self.cw[:, 0:8],
                                start=True, stop=True))

            @blk.vector
            def _(v):
                pe.wait(v)
                self.vs(v, v.tensor_copy(self.cwbc[:], pb[0][0:128, 0:8]))
                for e_ in range(NE):
                    sl = slice(e_ * 256, (e_ + 1) * 256)
                    self.vs(v, v.tensor_scalar(self.s1[:, 0:256], self.mu[:],
                                               self.cwbc[:, e_:e_ + 1], None, OP.mult))
                    self.vs(v, v.tensor_sub(self.s1[:, 0:256], self.scores[:, sl],
                                            self.s1[:, 0:256]))
                    self.vs(v, v.tensor_tensor(self.scores[:, sl], self.s1[:, 0:256],
                                               self.rstd[:], OP.mult))

        self.bar()

        with nc.Block() as blk:
            @blk.vector
            def _(v):
                self.vs(v, v.memset(self.lo8[:], -SBOUND))
                self.vs(v, v.memset(self.hi8[:], SBOUND))
        self.bar()
        for it in range(NSEARCH):
            with nc.Block() as blk:
                @blk.vector
                def _(v):
                    self.vs(v, v.tensor_add(self.t8[:], self.lo8[:], self.hi8[:]))
                    self.vs(v, v.tensor_scalar(self.mid8[:], self.t8[:], 0.5, None, OP.mult))

                @blk.tensor
                def _(t):
                    ve.wait(t)
                    midv = self.mid8[0:1, :].rearrange("p e -> p e ()").to_broadcast([1, 8, 256])
                    for n in range(4):
                        pe.inc(t.matmul(pb[n][:, :], self.onesr[:],
                                        midv[0:1, 2 * n:2 * n + 2, :],
                                        start=True, stop=True))

                @blk.vector
                def _(v):
                    pe.wait(v)
                    for n in range(4):
                        sl = slice(n * 512, (n + 1) * 512)
                        self.vs(v, v.tensor_tensor(self.srch[:, sl], self.scores[:, sl],
                                                   pb[n][:, :], OP.is_ge))
                    for e_ in range(NE):
                        self.vs(v, v.reduce_sum(self.cnt8[:, e_:e_ + 1],
                                                self.srch[:, e_ * 256:(e_ + 1) * 256],
                                                mybir.AxisListType.X))

                @blk.tensor
                def _(t):
                    ve.wait(t)
                    pe.inc(t.matmul(pb[4][0:1, 0:8], self.onescf[:], self.cnt8[:],
                                    start=True, stop=True))

                @blk.vector
                def _(v):
                    pe.wait(v)
                    self.vs(v, v.tensor_scalar(self.ge8[:], pb[4][0:1, 0:8],
                                               float(NTOK_G // 4), None, OP.is_ge))
                    self.vs(v, v.tensor_sub(self.t8[:], self.mid8[:], self.lo8[:]))
                    self.vs(v, v.tensor_tensor(self.t8[:], self.t8[:], self.ge8[:], OP.mult))
                    self.vs(v, v.tensor_add(self.lo8[:], self.lo8[:], self.t8[:]))
                    self.vs(v, v.tensor_sub(self.u8[:], self.hi8[:], self.mid8[:]))
                    self.vs(v, v.tensor_scalar(self.t8[:], self.ge8[:], -1.0, 1.0,
                                               OP.mult, OP.add))
                    self.vs(v, v.tensor_tensor(self.u8[:], self.u8[:], self.t8[:], OP.mult))
                    self.vs(v, v.tensor_sub(self.hi8[:], self.hi8[:], self.u8[:]))
        self.bar()
        with nc.Block() as blk:
            @blk.tensor
            def _(t):
                ve.wait(t)
                pe.inc(t.matmul(pb[0][0:128, 0:8], self.onesr[:], self.lo8[:],
                                start=True, stop=True))

            @blk.vector
            def _(v):
                pe.wait(v)
                self.vs(v, v.tensor_copy(self.th[:], pb[0][0:128, 0:8]))
        self.bar()

        sc2 = r(nc.sbuf_tensor("p2_sc2", [16, NE, 256], F32))
        mu2 = r(nc.sbuf_tensor("p2_mu2", [16, 256], F32))
        sq2 = r(nc.sbuf_tensor("p2_sq2", [16, 256], F32))
        rstd2 = r(nc.sbuf_tensor("p2_rstd2", [16, 256], F32))
        mask2 = r(nc.sbuf_tensor("p2_mask2", [16, 256], F32))
        tri16 = r(nc.sbuf_tensor("p2_tri16", [16, 16], F32))
        tri128 = r(nc.sbuf_tensor("p2_tri128", [128, 128], F32))
        colrow = r(nc.sbuf_tensor("p2_colrow", [1, 256], F32))
        csT = r(nc.sbuf_tensor("p2_csT", [128, 2], F32))
        excT = r(nc.sbuf_tensor("p2_excT", [128, 2], F32))
        tot2 = r(nc.sbuf_tensor("p2_tot2", [1, 2], F32))
        excrow = r(nc.sbuf_tensor("p2_excrow", [1, 256], F32))
        invf = r(nc.sbuf_tensor("p2_invf", [16, 256], F32))
        sgrid = r(nc.sbuf_tensor("p2_sgrid", [128, 10, 16], F32))
        with nc.Block() as blk:
            @blk.sync
            def _(sy):
                ld.inc(sy.dma_start(tri16[:], self.CST["tri16s_f"]), 16)
                ld.inc(sy.dma_start(tri128[:], self.CST["tri128s_f"]), 16)
        for i in range(2):
            half = slice(128 * i, 128 * (i + 1))
            with nc.Block() as blk:
                @blk.sync
                def _(sy, i=i):
                    ve.wait(sy)
                    ld.inc(sy.dma_start(
                        sgrid[:], self.sc_hbm[i].rearrange("r (b a) -> b r a", a=16)), 16)

                @blk.tensor
                def _(t, i=i):
                    ld.wait(t)
                    ve.wait(t)
                    for rr in range(10):
                        ins = t.transpose(
                            pb[rr // 4][0:16, 128 * (rr % 4):128 * (rr % 4) + 128],
                            sgrid[:, rr, :], self.ident[:])
                        if rr == 9:
                            pe.inc(ins)

                @blk.vector
                def _(v, i=i, half=half):
                    pe.wait(v)
                    for e_ in range(NE):
                        self.vs(v, v.tensor_copy(
                            sc2[0:16, e_, half],
                            pb[e_ // 4][0:16, 128 * (e_ % 4):128 * (e_ % 4) + 128]))
                    self.vs(v, v.tensor_copy(mu2[:, half], pb[2][0:16, 0:128]))
                    self.vs(v, v.tensor_copy(sq2[:, half], pb[2][0:16, 128:256]))
        with nc.Block() as blk:
            @blk.vector
            def _(v):
                ld.wait(v)
                self.vs(v, v.tensor_scalar(mu2[:], mu2[:], 1.0 / D, None, OP.mult))
                self.vs(v, v.tensor_tensor(rstd2[:], mu2[:], mu2[:], OP.mult))
                self.vs(v, v.scalar_tensor_tensor(rstd2[:], sq2[:], 1.0 / D,
                                                  rstd2[:], OP.mult, OP.subtract))

            @blk.scalar
            def _(s):
                ve.wait(s)
                ac.inc(s.activation(rstd2[:], rstd2[:], AF.Sqrt, bias=self.epsc[0:16, 0:1]))

            @blk.vector
            def _(v):
                ac.wait(v)
                self.vs(v, v.reciprocal(rstd2[:], rstd2[:]))
                for e_ in range(NE):
                    self.vs(v, v.tensor_scalar(self.s1[0:16, 0:256], mu2[:],
                                               self.cwbc[0:16, e_:e_ + 1], None, OP.mult))
                    self.vs(v, v.tensor_sub(self.s1[0:16, 0:256], sc2[0:16, e_, :],
                                            self.s1[0:16, 0:256]))
                    self.vs(v, v.tensor_tensor(sc2[0:16, e_, :], self.s1[0:16, 0:256],
                                               rstd2[:], OP.mult))
        self.bar()

        with nc.Block() as blk:
            @blk.vector
            def _(v):
                self.vs(v, v.memset(self.lists.ap().rearrange("p a b -> p (a b)"), 0))
        self.bar()
        for e_ in range(NE):
            with nc.Block() as blk:
                @blk.scalar
                def _(s, e_=e_):
                    ve.wait(s)
                    gp.wait(s)
                    st[0].wait(s)
                    ac.inc(s.activation(self.gmy[:], sc2[0:16, e_, :], AF.Sigmoid))

                @blk.vector
                def _(v, e_=e_):
                    gp.wait(v)
                    ac.wait(v)
                    self.vs(v, v.tensor_scalar(self.m01[:], sc2[0:16, e_, :],
                                               self.th[0:16, e_:e_ + 1], None, OP.is_ge))
                    self.vs(v, v.tensor_scalar(self.mif[:], self.iota16[:], 1.0, None, OP.add))
                    self.vs(v, v.tensor_tensor(self.mif[:], self.m01[:], self.mif[:], OP.mult))
                    self.vs(v, v.tensor_scalar(self.mif[:], self.mif[:], 1.0, None, OP.subtract))
                    self.vs(v, v.tensor_copy(self.mii[:], self.mif[:]))
                    self.vs(v, v.tensor_scalar(self.mif[:], self.gmy[:], 1.0, None, OP.add))
                    self.vs(v, v.tensor_tensor(self.mif[:], self.m01[:], self.mif[:], OP.mult))
                    self.vs(v, v.tensor_scalar(self.mif[:], self.mif[:], 1.0, None, OP.subtract))

                @blk.gpsimd
                def _(g, e_=e_):
                    if e_ == 0:
                        g.load_library(LIB_SG)
                    ve.wait(g)
                    gp.inc(g.sparse_gather(self.lists[0:16, e_, :], self.mii[:],
                                           num_found=self.nf[0:1, e_:e_ + 1]))
                    gp.inc(g.sparse_gather(glist[:], self.mif[:],
                                           num_found=self.nf2[0:1, e_:e_ + 1]))

                @blk.sync
                def _(sy, e_=e_):
                    gp.wait(sy)
                    for pp in range(8):
                        st[0].inc(sy.dma_start(
                            self.gv9[16 * pp:16 * pp + 16, e_, :],
                            glist[:, pp:pp + 72:8][:, 0:9],
                            allow_slow_non_contiguous=True), 16)

            with nc.Block() as blk:
                @blk.vector
                def _(v, e_=e_):
                    gp.wait(v)
                    self.vs(v, v.tensor_copy(invf[:], self.lists[0:16, e_, :]))
                    self.vs(v, v.tensor_scalar(invf[:], invf[:], 0.0, None, OP.max))
                    self.vs(v, v.tensor_copy(self.lists[0:16, e_, :], invf[:]))
                    self.vs(v, v.tensor_copy(mask2[:], self.m01[:]))

                @blk.tensor
                def _(t, e_=e_):
                    ve.wait(t)
                    pe.inc(t.matmul(pb[2][0:1, 0:256], self.onescf[0:16, 0:1],
                                    mask2[:], start=True, stop=True))

                @blk.vector
                def _(v, e_=e_):
                    pe.wait(v)
                    self.vs(v, v.tensor_copy(colrow[:], pb[2][0:1, 0:256]))

                @blk.tensor
                def _(t, e_=e_):
                    ve.wait(t)
                    t.transpose(pb[3][0:128, 0:1], colrow[0:1, 0:128],
                                self.ident[0:1, 0:1])
                    pe.inc(t.transpose(pb[3][0:128, 1:2], colrow[0:1, 128:256],
                                       self.ident[0:1, 0:1]))

                @blk.vector
                def _(v, e_=e_):
                    pe.wait(v)
                    self.vs(v, v.tensor_copy(csT[:], pb[3][0:128, 0:2]))

                @blk.tensor
                def _(t, e_=e_):
                    ve.wait(t)
                    pe.inc(t.matmul(pb[4][0:128, 0:2], tri128[:], csT[:],
                                    start=True, stop=True))
                    pe.inc(t.matmul(pb[5][0:1, 0:2], self.onescf[:], csT[:],
                                    start=True, stop=True))

                @blk.vector
                def _(v, e_=e_):
                    pe.wait(v)
                    self.vs(v, v.tensor_copy(excT[:], pb[4][0:128, 0:2]))
                    self.vs(v, v.tensor_copy(tot2[:], pb[5][0:1, 0:2]))

                @blk.tensor
                def _(t, e_=e_):
                    ve.wait(t)
                    pe.inc(t.matmul(pb[6][0:128, 0:1], self.onesr[:], tot2[0:1, 0:1],
                                    start=True, stop=True))

                @blk.vector
                def _(v, e_=e_):
                    pe.wait(v)
                    self.vs(v, v.tensor_add(excT[:, 1:2], excT[:, 1:2], pb[6][0:128, 0:1]))

                @blk.tensor
                def _(t, e_=e_):
                    ve.wait(t)
                    t.transpose(pb[7][0:1, 0:128], excT[:, 0:1], self.ident[:])
                    pe.inc(t.transpose(pb[7][0:1, 128:256], excT[:, 1:2], self.ident[:]))

                @blk.vector
                def _(v, e_=e_):
                    pe.wait(v)
                    self.vs(v, v.tensor_copy(excrow[:], pb[7][0:1, 0:256]))

                @blk.tensor
                def _(t, e_=e_):
                    ve.wait(t)
                    t.matmul(pb[0][0:16, 0:256], tri16[:], mask2[:],
                             start=True, stop=False)
                    pe.inc(t.matmul(pb[0][0:16, 0:256], self.onesr[0:1, 0:16],
                                    excrow[:], start=False, stop=True))

                @blk.vector
                def _(v, e_=e_):
                    pe.wait(v)
                    self.vs(v, v.tensor_scalar(invf[:], pb[0][0:16, 0:256],
                                               -float(CAP), None, OP.add))
                    self.vs(v, v.tensor_tensor(invf[:], mask2[:], invf[:], OP.mult))
                    self.vs(v, v.tensor_scalar(invf[:], invf[:], float(CAP), None, OP.add))
                    self.vs(v, v.tensor_scalar(invf[:], invf[:], float(CAP), None, OP.min))
                    self.vs(v, v.tensor_copy(self.inv16[0:16, e_, :], invf[:]))
        self.bar()
        with nc.Block() as blk:
            @blk.vector
            def _(v):
                st[0].wait(v)
                self.vs(v, v.tensor_scalar(self.gv9.ap().rearrange("p a b -> p (a b)"),
                                           self.gv9.ap().rearrange("p a b -> p (a b)"),
                                           0.0, None, OP.max))

            @blk.gpsimd
            def _(g):
                ve.wait(g)
                for gix in range(1, 8):
                    sl = slice(16 * gix, 16 * gix + 16)
                    sgd.inc(g.dma_start(self.lists[sl, :, :], self.lists[0:16, :, :]), 16)
                    sgd.inc(g.dma_start(self.inv16[sl, :, :], self.inv16[0:16, :, :]), 16)
                sgd.wait(g)
        self.bar()
        R.close()

    def p3a_scan(self):
        nc, din = self.nc, self.din
        ld, lda, lsw, pe, ve, ac, st, trl = (self.ld, self.lda, self.lsw, self.pe,
                                             self.ve, self.ac, self.st, self.trl)
        R = ExitStack()
        r = R.enter_context
        wri = r(nc.sbuf_tensor("p3_wri", [128, DC, 2048], BF16))
        wo = r(nc.sbuf_tensor("p3_wo", [128, DC, 1024], BF16))
        tGr = r(nc.sbuf_tensor("p3_tGr", [128, NB, D], BF16))
        tGi = r(nc.sbuf_tensor("p3_tGi", [128, NB, D], BF16))
        tLr = r(nc.sbuf_tensor("p3_tLr", [128, DC, 512], BF16))
        tLi = r(nc.sbuf_tensor("p3_tLi", [128, DC, 512], BF16))
        xnsl = r(nc.sbuf_tensor("p3_xnsl", [128, DC, 512], BF16))
        utr = r(nc.sbuf_tensor("p3_utr", [128, NB, D], BF16))
        uti = r(nc.sbuf_tensor("p3_uti", [128, NB, D], BF16))
        hbuf = r(nc.sbuf_tensor("p3_hbuf", [128, 2, DC, 512], BF16))
        osm = r(nc.sbuf_tensor("p3_osm", [128, 2, 512], BF16))
        wcol = r(nc.sbuf_tensor("p3_wcol", [128, 2, 8], F32))
        prow = r(nc.sbuf_tensor("p3_prow", [1, 3 * D], F32))
        pval = r(nc.sbuf_tensor("p3_pval", [1, 3 * D], F32))
        brow = r(nc.sbuf_tensor("p3_brow", [1, 2 * D], F32))
        iota512 = r(nc.sbuf_tensor("p3_iota512", [1, 512], F32))
        upat = r(nc.sbuf_tensor("p3_upat", [128, NB, 512], BF16))
        erow = r(nc.sbuf_tensor("p3_erow", [8, DC * 512], BF16))
        nu8 = r(nc.sbuf_tensor("p3_nu8", [8, 128], F32))
        th8r = r(nc.sbuf_tensor("p3_th8r", [8, 128], F32))
        lamC8 = r(nc.sbuf_tensor("p3_lamC8", [8, 2, 128], F32))
        w8T = r(nc.sbuf_tensor("p3_w8T", [8, 2, 128], BF16))
        s8a = r(nc.sbuf_tensor("p3_s8a", [8, 128], F32))
        s8b = r(nc.sbuf_tensor("p3_s8b", [8, 128], F32))
        pb = [r(nc.psum_tensor(f"p3_pb{i}", [128, 512], F32)) for i in range(8)]

        with nc.Block() as blk:
            @blk.sync
            def _(sy):
                ld.inc(sy.dma_start(iota512[:], self.CST["iota512_f"]), 16)
                for q in range(NB):
                    ld.inc(sy.dma_start(upat[:, q, :], self.CST[f"upat{q}_b"]), 16)
                ld.inc(sy.dma_start(erow[:], self.CST["erow_b"]), 16)
                for i in range(3):
                    ld.inc(sy.dma_start(prow[:, i * D:(i + 1) * D], din["plog"][i:i + 1, :]), 16)
                for i in range(2):
                    ld.inc(sy.dma_start(brow[:, i * D:(i + 1) * D], din["bin_"][i:i + 1, :]), 16)
                ld.inc(sy.dma_start(nu8[:], din["plog"][0:1, :].rearrange("a (p n) -> (a p) n", p=8)), 16)
                ld.inc(sy.dma_start(th8r[:], din["plog"][1:2, :].rearrange("a (p n) -> (a p) n", p=8)), 16)

            @blk.gpsimd
            def _(g):
                g.load_library(LIB_STD)
                for k in range(DC):
                    lsw.inc(g.dma_start(wri[:, k, 0:1024],
                                        din["Win"][0, k * 128:(k + 1) * 128, :]), 16)
                    lsw.inc(g.dma_start(wri[:, k, 1024:2048],
                                        din["Win"][1, k * 128:(k + 1) * 128, :]), 16)
                    lsw.inc(g.dma_start(wo[:, k, :],
                                        din["Win"][2, k * 128:(k + 1) * 128, :]), 16)

            @blk.scalar
            def _(s):
                ld.wait(s)
                ac.inc(s.activation(pval[:], prow[:], AF.Exp))
        self.bar()

        for q in range(NB):
            with nc.Block() as blk:
                @blk.scalar
                def _(s, q=q):
                    ac.inc(s.activation(self.s1[0:1, 0:D], pval[:, 0:D], AF.Copy,
                                        scale=float(128 * q)))
                    ac.inc(s.activation(self.s1[0:1, 1024:2048], pval[:, D:2 * D],
                                        AF.Copy, scale=float(128 * q)))

                @blk.tensor
                def _(t, q=q):
                    ac.wait(t)
                    ld.wait(t)
                    iot = iota512[0:1, 0:128]
                    for half in range(2):
                        sl = slice(half * 512, half * 512 + 512)
                        t.matmul(pb[half][:, :], iot, pval[:, 0:D][:, sl],
                                 start=True, stop=False)
                        pe.inc(t.matmul(pb[half][:, :], self.onesr[:],
                                        self.s1[0:1, 0:D][:, sl], start=False, stop=True))
                        t.matmul(pb[2 + half][:, :], iot, pval[:, D:2 * D][:, sl],
                                 start=True, stop=False)
                        pe.inc(t.matmul(pb[2 + half][:, :], self.onesr[:],
                                        self.s1[0:1, 1024:2048][:, sl], start=False, stop=True))

                @blk.vector
                def _(v, q=q):
                    pe.wait(v)
                    for half in range(2):
                        hs = slice(half * 512, half * 512 + 512)
                        self.vs(v, v.tensor_scalar(self.s2[:, 0:1024][:, hs], pb[2 + half][:, :],
                                                   1.0 / (2 * PI), TWO23, OP.mult, OP.add))
                        self.vs(v, v.tensor_scalar(self.s2[:, 0:1024][:, hs],
                                                   self.s2[:, 0:1024][:, hs],
                                                   TWO23, None, OP.subtract))
                        self.vs(v, v.scalar_tensor_tensor(self.s2[:, 0:1024][:, hs],
                                                          self.s2[:, 0:1024][:, hs],
                                                          -2 * PI, pb[2 + half][:, :],
                                                          OP.mult, OP.add))
                    self.vs(v, v.tensor_scalar(self.s2[:, 0:1024], self.s2[:, 0:1024],
                                               -PI, PI, OP.max, OP.min))

                @blk.scalar
                def _(s, q=q):
                    ve.wait(s)
                    ac.inc(s.activation(self.s1[:, 0:512], pb[0][:, :], AF.Exp))
                    ac.inc(s.activation(self.s1[:, 512:1024], pb[1][:, :], AF.Exp))
                    ac.inc(s.activation(self.s2[:, 1024:2048], self.s2[:, 0:1024], AF.Sin))
                    ac.inc(s.activation(self.s2[:, 0:1024], self.s2[:, 0:1024], AF.Sin,
                                        scale=0.5))

                @blk.tensor
                def _(t, q=q):
                    ac.wait(t)
                    for half in range(2):
                        sl = slice(half * 512, half * 512 + 512)
                        pe.inc(t.matmul(pb[4 + half][:, :], self.onesr[:],
                                        pval[:, 2 * D:3 * D][:, sl], start=True, stop=True))

                @blk.vector
                def _(v, q=q):
                    pe.wait(v)
                    self.vs(v, v.tensor_tensor(self.s2[:, 0:1024], self.s2[:, 0:1024],
                                               self.s2[:, 0:1024], OP.mult))
                    self.vs(v, v.tensor_scalar(self.s2[:, 0:1024], self.s2[:, 0:1024],
                                               -2.0, 1.0, OP.mult, OP.add))
                    for half in range(2):
                        hs = slice(half * 512, half * 512 + 512)
                        self.vs(v, v.tensor_tensor(self.s1[:, 0:1024][:, hs],
                                                   self.s1[:, 0:1024][:, hs],
                                                   pb[4 + half][:, :], OP.mult))
                    self.vs(v, v.tensor_tensor(tGr[:, q, :], self.s1[:, 0:1024],
                                               self.s2[:, 0:1024], OP.mult))
                    self.vs(v, v.tensor_tensor(self.s1[:, 1024:2048], self.s1[:, 0:1024],
                                               self.s2[:, 1024:2048], OP.mult))
                    self.vs(v, v.tensor_scalar(tGi[:, q, :], self.s1[:, 1024:2048],
                                               -1.0, None, OP.mult))

        for dc in range(DC):
            with nc.Block() as blk:
                @blk.tensor
                def _(t, dc=dc):
                    nsl = slice(dc * 128, (dc + 1) * 128)
                    pe.inc(t.matmul(pb[0][:, :], pval[0:1, 0:D][:, nsl],
                                    iota512[:], start=True, stop=True))
                    pe.inc(t.matmul(pb[1][:, :], pval[0:1, D:2 * D][:, nsl],
                                    iota512[:], start=True, stop=True))

                @blk.vector
                def _(v, dc=dc):
                    pe.wait(v)
                    self.vs(v, v.tensor_scalar(self.s2[:, 0:512], pb[1][:, :],
                                               1.0 / (2 * PI), TWO23, OP.mult, OP.add))
                    self.vs(v, v.tensor_scalar(self.s2[:, 0:512], self.s2[:, 0:512],
                                               TWO23, None, OP.subtract))
                    self.vs(v, v.scalar_tensor_tensor(self.s2[:, 0:512], self.s2[:, 0:512],
                                                      -2 * PI, pb[1][:, :],
                                                      OP.mult, OP.add))
                    self.vs(v, v.tensor_scalar(self.s2[:, 0:512], self.s2[:, 0:512],
                                               -PI, PI, OP.max, OP.min))

                @blk.scalar
                def _(s, dc=dc):
                    ve.wait(s)
                    ac.inc(s.activation(self.s1[:, 0:512], pb[0][:, :], AF.Exp, scale=-1.0))
                    ac.inc(s.activation(self.s2[:, 512:1024], self.s2[:, 0:512], AF.Sin))
                    ac.inc(s.activation(self.s2[:, 0:512], self.s2[:, 0:512], AF.Sin, scale=0.5))

                @blk.vector
                def _(v, dc=dc):
                    ac.wait(v)
                    self.vs(v, v.tensor_tensor(self.s2[:, 0:512], self.s2[:, 0:512],
                                               self.s2[:, 0:512], OP.mult))
                    self.vs(v, v.tensor_scalar(self.s2[:, 0:512], self.s2[:, 0:512],
                                               -2.0, 1.0, OP.mult, OP.add))
                    self.vs(v, v.tensor_tensor(tLr[:, dc, :], self.s1[:, 0:512],
                                               self.s2[:, 0:512], OP.mult))
                    self.vs(v, v.tensor_tensor(tLi[:, dc, :], self.s1[:, 0:512],
                                               self.s2[:, 512:1024], OP.mult))
            self.bar()

        with nc.Block() as blk:
            @blk.scalar
            def _(s):
                ld.wait(s)
                ac.inc(s.activation(s8a[:], nu8[:], AF.Exp))
                ac.inc(s.activation(s8b[:], th8r[:], AF.Exp))

            @blk.vector
            def _(v):
                ac.wait(v)
                self.vs(v, v.tensor_scalar(s8b[:], s8b[:], float(CH), None, OP.mult))
                self.vs(v, v.tensor_scalar(lamC8[:, 0, :], s8b[:],
                                           1.0 / (2 * PI), TWO23, OP.mult, OP.add))
                self.vs(v, v.tensor_scalar(lamC8[:, 0, :], lamC8[:, 0, :],
                                           TWO23, None, OP.subtract))
                self.vs(v, v.scalar_tensor_tensor(s8b[:], lamC8[:, 0, :],
                                                  -2 * PI, s8b[:], OP.mult, OP.add))
                self.vs(v, v.tensor_scalar(s8b[:], s8b[:], -PI, PI, OP.max, OP.min))

            @blk.scalar
            def _(s):
                ve.wait(s)
                ac.inc(s.activation(lamC8[:, 1, :], s8b[:], AF.Sin))
                ac.inc(s.activation(s8b[:], s8b[:], AF.Sin, scale=0.5))
                ac.inc(s.activation(s8a[:], s8a[:], AF.Copy, scale=float(-CH)))
                s.wait_ge(ac.sem, ac.n)
                ac.inc(s.activation(s8a[:], s8a[:], AF.Exp))

            @blk.vector
            def _(v):
                ac.wait(v)
                self.vs(v, v.tensor_tensor(lamC8[:, 0, :], s8b[:], s8b[:], OP.mult))
                self.vs(v, v.tensor_scalar(lamC8[:, 0, :], lamC8[:, 0, :],
                                           -2.0, 1.0, OP.mult, OP.add))
                self.vs(v, v.tensor_tensor(lamC8[:, 0, :], lamC8[:, 0, :], s8a[:], OP.mult))
                self.vs(v, v.tensor_tensor(lamC8[:, 1, :], lamC8[:, 1, :], s8a[:], OP.mult))
        self.bar()

        for kc in range(NCHUNK):
            tok0 = kc * CH
            par = kc % 2
            seq_start = (kc % (NCHUNK // 2) == 0)
            with nc.Block() as blk:
                @blk.sync
                def _(sy, kc=kc, tok0=tok0):
                    pe.wait(sy)
                    for k in range(DC):
                        trl[0].inc(sy.dma_start(
                            xnsl[:, k, :],
                            self.xn_hbm[tok0:tok0 + CH, k * 128:(k + 1) * 128],
                            transpose=True), 16)

                @blk.vector
                def _(v, seq_start=seq_start):
                    if seq_start:
                        self.vs(v, v.memset(w8T[:], 0.0))

            for q in range(NB):
                with nc.Block() as blk:
                    @blk.tensor
                    def _(t, kc=kc, q=q):
                        trl[0].wait(t)
                        lsw.wait(t)
                        ld.wait(t)
                        ve.wait(t)
                        for k in range(DC):
                            for n in range(4):
                                sl = slice(n * 512, (n + 1) * 512)
                                t.matmul(pb[n][:, :],
                                         xnsl[:, k, 128 * q:128 * (q + 1)],
                                         wri[:, k, sl], start=(k == 0), stop=False)
                        for n in range(4):
                            sl = slice(n * 512, (n + 1) * 512)
                            ins = t.matmul(pb[n][:, :], self.onesr[:],
                                           brow[:, sl], start=False, stop=True)
                        pe.inc(ins)

                    @blk.vector
                    def _(v, kc=kc, q=q):
                        pe.wait(v)
                        for hf in range(2):
                            hs = slice(hf * 512, hf * 512 + 512)
                            ur = pb[hf][:, :]
                            ui = pb[2 + hf][:, :]
                            self.vs(v, v.tensor_tensor(self.s1[:, 0:1024][:, hs],
                                                       tGr[:, q, hs], ur, OP.mult))
                            self.vs(v, v.tensor_tensor(self.s2[:, 0:1024][:, hs],
                                                       tGi[:, q, hs], ui, OP.mult))
                        self.vs(v, v.tensor_sub(utr[:, q, :], self.s1[:, 0:1024],
                                                self.s2[:, 0:1024]))
                        for hf in range(2):
                            hs = slice(hf * 512, hf * 512 + 512)
                            ur = pb[hf][:, :]
                            ui = pb[2 + hf][:, :]
                            self.vs(v, v.tensor_tensor(self.s1[:, 0:1024][:, hs],
                                                       tGr[:, q, hs], ui, OP.mult))
                            self.vs(v, v.tensor_tensor(self.s2[:, 0:1024][:, hs],
                                                       tGi[:, q, hs], ur, OP.mult))
                        self.vs(v, v.tensor_add(uti[:, q, :], self.s1[:, 0:1024],
                                                self.s2[:, 0:1024]))

            for dc in range(DC):
                with nc.Block() as blk:
                    @blk.tensor
                    def _(t, kc=kc, dc=dc):
                        ve.wait(t)
                        for comp, psl in ((0, pb[4][:, :]), (1, pb[5][:, :])):
                            ut = utr if comp == 0 else uti
                            for q in range(NB):
                                t.matmul(psl, ut[:, q, 128 * dc:128 * (dc + 1)],
                                         upat[:, q, :], start=(q == 0), stop=False)
                            ins = t.matmul(psl, w8T[:, comp, :],
                                           erow[:, dc * 512:(dc + 1) * 512],
                                           start=False, stop=True)
                            if comp == 1:
                                pe.inc(ins)

                    @blk.vector
                    def _(v, kc=kc, par=par, dc=dc):
                        pe.wait(v)
                        if dc == 0:
                            st[0].wait(v)
                        cr = pb[4][:, :]
                        ci = pb[5][:, :]
                        self.vs(v, v.tensor_tensor(self.s1[:, 0:512], tLr[:, dc, :], cr, OP.mult))
                        self.vs(v, v.tensor_tensor(self.s2[:, 0:512], tLi[:, dc, :], ci, OP.mult))
                        self.vs(v, v.tensor_sub(hbuf[:, 0, dc, :], self.s1[:, 0:512],
                                                self.s2[:, 0:512]))
                        self.vs(v, v.tensor_tensor(self.s1[:, 0:512], tLr[:, dc, :], ci, OP.mult))
                        self.vs(v, v.tensor_tensor(self.s2[:, 0:512], tLi[:, dc, :], cr, OP.mult))
                        self.vs(v, v.tensor_add(hbuf[:, 1, dc, :], self.s1[:, 0:512],
                                                self.s2[:, 0:512]))
                        self.vs(v, v.tensor_copy(wcol[:, 0, dc:dc + 1], cr[:, 511:512]))
                        self.vs(v, v.tensor_copy(wcol[:, 1, dc:dc + 1], ci[:, 511:512]))

            with nc.Block() as blk:
                @blk.tensor
                def _(t, kc=kc):
                    ve.wait(t)
                    t.transpose(pb[6][0:8, 0:128], wcol[:, 0, :], self.ident[:])
                    pe.inc(t.transpose(pb[6][0:8, 128:256], wcol[:, 1, :], self.ident[:]))

                @blk.vector
                def _(v, kc=kc):
                    pe.wait(v)
                    tr = pb[6][0:8, 0:128]
                    ti = pb[6][0:8, 128:256]
                    self.vs(v, v.tensor_tensor(s8a[:], lamC8[:, 0, :], tr, OP.mult))
                    self.vs(v, v.tensor_tensor(s8b[:], lamC8[:, 1, :], ti, OP.mult))
                    self.vs(v, v.tensor_sub(w8T[:, 0, :], s8a[:], s8b[:]))
                    self.vs(v, v.tensor_tensor(s8a[:], lamC8[:, 0, :], ti, OP.mult))
                    self.vs(v, v.tensor_tensor(s8b[:], lamC8[:, 1, :], tr, OP.mult))
                    self.vs(v, v.tensor_add(w8T[:, 1, :], s8a[:], s8b[:]))

            for m in range(DC):
                with nc.Block() as blk:
                    @blk.tensor
                    def _(t, kc=kc, m=m):
                        ac.wait(t)
                        for k in range(DC):
                            ins = t.matmul(pb[7][:, :],
                                           wo[:, k, 128 * m:128 * (m + 1)],
                                           xnsl[:, k, :], start=(k == 0),
                                           stop=(k == DC - 1))
                        pe.inc(ins)

                    @blk.scalar
                    def _(s, kc=kc, m=m, par=par):
                        pe.wait(s)
                        st[2 + m % 2].wait(s)
                        ac.inc(s.activation(osm[:, m % 2, :], pb[7][:, :], AF.Copy))

                    @blk.sync
                    def _(sy, kc=kc, m=m, tok0=tok0):
                        ac.wait(sy)
                        st[2 + m % 2].inc(sy.dma_start(self.ho_hbm[2, m, :, tok0:tok0 + CH],
                                                       osm[:, m % 2, :]), 16)

            with nc.Block() as blk:
                @blk.sync
                def _(sy, kc=kc, par=par, tok0=tok0):
                    ve.wait(sy)
                    for dc in range(DC):
                        st[0].inc(sy.dma_start(self.ho_hbm[0, dc, :, tok0:tok0 + CH],
                                               hbuf[:, 0, dc, :]), 16)
                        st[0].inc(sy.dma_start(self.ho_hbm[1, dc, :, tok0:tok0 + CH],
                                               hbuf[:, 1, dc, :]), 16)
        with nc.Block() as blk:
            @blk.sync
            def _(sy):
                for j in range(4):
                    st[j].wait(sy)
        self.bar()
        R.close()

    def p3b_tail(self):
        nc, din = self.nc, self.din
        ld, lda, lsw, pe, ve, ac, st = (self.ld, self.lda, self.lsw, self.pe,
                                        self.ve, self.ac, self.st)
        R = ExitStack()
        r = R.enter_context
        wm = r(nc.sbuf_tensor("p3b_wm", [128, DC, 2048], BF16))
        w1s = r(nc.sbuf_tensor("p3b_w1", [128, DC, 1024], BF16))
        w2s = r(nc.sbuf_tensor("p3b_w2", [128, DC, 1024], BF16))
        hsl = r(nc.sbuf_tensor("p3b_hsl", [128, 2, DC, 512], BF16))
        osl = r(nc.sbuf_tensor("p3b_osl", [128, DC, 512], BF16))
        xt2 = r(nc.sbuf_tensor("p3b_xt2", [128, 2, 1024], F32))
        bmd = r(nc.sbuf_tensor("p3b_bmd", [1, D], F32))
        bm1 = r(nc.sbuf_tensor("p3b_bm1", [1, D], F32))
        w1brow = r(nc.sbuf_tensor("p3b_w1b", [1, D], F32))
        w2brow = r(nc.sbuf_tensor("p3b_w2b", [1, D], F32))
        lngr = r(nc.sbuf_tensor("p3b_lng", [1, D], F32))
        lnbr = r(nc.sbuf_tensor("p3b_lnb", [1, D], F32))
        outb = r(nc.sbuf_tensor("p3b_outb", [128, DC, 512], BF16))
        y1b = r(nc.sbuf_tensor("p3b_y1b", [128, DC, 512], BF16))
        yb = r(nc.sbuf_tensor("p3b_yb", [128, DC, 512], BF16))
        gbc = r(nc.sbuf_tensor("p3b_gbc", [128, 1024], BF16))
        bbc = r(nc.sbuf_tensor("p3b_bbc", [128, 1024], BF16))
        x1t = r(nc.sbuf_tensor("p3b_x1t", [128, 2, 1024], F32))
        xadd = r(nc.sbuf_tensor("p3b_xadd", [128, NE, 1024], BF16))
        c128 = r(nc.sbuf_tensor("p3b_c128", [1, 1], U32))
        pb = [r(nc.psum_tensor(f"p3b_pb{i}", [128, 512], F32)) for i in range(6)]
        psb = r(nc.psum_tensor("p3b_psb", [128, 1024], BF16))
        sgd = self.sgd
        gcnt = {}
        acnt = {}
        r128 = {}

        with nc.Block() as blk:
            @blk.sync
            def _(sy):
                ld.inc(sy.dma_start(bmd[:], din["bmid"][0:1, :]), 16)
                ld.inc(sy.dma_start(bm1[:], din["bmid"][1:2, :]), 16)
                ld.inc(sy.dma_start(w1brow[:], din["w1b"][:]), 16)
                ld.inc(sy.dma_start(w2brow[:], din["w2b"][:]), 16)
                ld.inc(sy.dma_start(lngr[:], din["lng"][:]), 16)
                ld.inc(sy.dma_start(lnbr[:], din["lnb"][:]), 16)

            @blk.gpsimd
            def _(g):
                for k in range(DC):
                    lsw.inc(g.dma_start(wm[:, k, 0:1024],
                                        din["Wmid"][0, k * 128:(k + 1) * 128, :]), 16)
                    lsw.inc(g.dma_start(wm[:, k, 1024:2048],
                                        din["Wmid"][1, k * 128:(k + 1) * 128, :]), 16)
                    lsw.inc(g.dma_start(w1s[:, k, :], din["w1w"][k * 128:(k + 1) * 128, :]), 16)
                    lsw.inc(g.dma_start(w2s[:, k, :], din["w2w"][k * 128:(k + 1) * 128, :]), 16)

            @blk.vector
            def _(v):
                lsw.wait(v)
                ld.wait(v)
                self.vs(v, v.tensor_sub(bmd[:], bmd[:], bm1[:]))
                for k in range(DC):
                    self.ve.inc(v.tensor_scalar(wm[:, k, 1024:2048], wm[:, k, 1024:2048],
                                                -1.0, None, OP.mult))
                v.wait_ge(ve.sem, ve.n)

            @blk.tensor
            def _(t):
                ve.wait(t)
                for half in range(2):
                    sl = slice(half * 512, half * 512 + 512)
                    pe.inc(t.matmul(pb[half][:, :], self.onesr[:],
                                    lngr[:, sl], start=True, stop=True))
                    pe.inc(t.matmul(pb[2 + half][:, :], self.onesr[:],
                                    lnbr[:, sl], start=True, stop=True))

            @blk.vector
            def _(v):
                pe.wait(v)
                for half in range(2):
                    sl = slice(half * 512, half * 512 + 512)
                    self.vs(v, v.tensor_copy(gbc[:, sl], pb[half][:, :]))
                    self.vs(v, v.tensor_copy(bbc[:, sl], pb[2 + half][:, :]))

            @blk.vector
            def _(v):
                self.vs(v, v.memset(c128[:], 128))

            @blk.gpsimd
            def _(g_):
                ve.wait(g_)
                lsw.wait(g_)
                g_.load_library(LIB_MLP)
                r128["r"] = g_.alloc_register("r_c128")
                g_.load(r128["r"], c128[0:1, 0:1])
                for e_ in range(NE):
                    sgd.inc(g_.dma_gather(xadd[:, e_:e_ + 1, :], self.moebuf[e_],
                                          self.inv16[:, e_, 0:8], 128, r128["r"], D,
                                          elem_step=D), 16)
                gcnt[0] = sgd.n
        self.bar()

        for kc in range(NCHUNK):
            tok0 = kc * CH
            par = kc % 2
            with nc.Block() as blk:
                @blk.scalar
                def _(s, kc=kc, par=par, tok0=tok0):
                    pe.wait(s)
                    for dc in range(DC):
                        lda.inc(s.dma_start(hsl[:, 0, dc, :],
                                            self.ho_hbm[0, dc, :, tok0:tok0 + CH]), 16)
                        lda.inc(s.dma_start(hsl[:, 1, dc, :],
                                            self.ho_hbm[1, dc, :, tok0:tok0 + CH]), 16)
                        lda.inc(s.dma_start(osl[:, dc, :],
                                            self.ho_hbm[2, dc, :, tok0:tok0 + CH]), 16)

            for m in range(DC):
                with nc.Block() as blk:
                    @blk.tensor
                    def _(t, kc=kc, par=par, m=m):
                        lda.wait(t)
                        lsw.wait(t)
                        ve.wait(t)
                        bank = pb[m % 2][:, :]
                        for k in range(DC):
                            t.matmul(bank, wm[:, k, 128 * m:128 * (m + 1)],
                                     hsl[:, 0, k, :], start=(k == 0), stop=False)
                            t.matmul(bank, wm[:, k, 1024 + 128 * m:1024 + 128 * (m + 1)],
                                     hsl[:, 1, k, :], start=False, stop=False)
                        pe.inc(t.matmul(bank, bmd[:, 128 * m:128 * (m + 1)],
                                        self.ones512[:], start=False, stop=True))

                    @blk.vector
                    def _(v, kc=kc, par=par, m=m):
                        pe.wait(v)
                        bank = pb[m % 2][:, :]
                        self.vs(v, v.tensor_add(outb[:, m, :], bank, osl[:, m, :]))

            for m in range(DC):
                with nc.Block() as blk:
                    @blk.tensor
                    def _(t, m=m):
                        ve.wait(t)
                        ac.wait(t)
                        bank = pb[2 + m % 2][:, :]
                        for k in range(DC):
                            t.matmul(bank, w1s[:, k, 128 * m:128 * (m + 1)], outb[:, k, :],
                                     start=(k == 0), stop=False)
                        pe.inc(t.matmul(bank, w1brow[:, 128 * m:128 * (m + 1)],
                                        self.ones512[:], start=False, stop=True))

                    @blk.scalar
                    def _(s, m=m):
                        pe.wait(s)
                        bank = pb[2 + m % 2][:, :]
                        ac.inc(s.activation(y1b[:, m, :], bank, AF.Gelu))

            for m in range(DC):
                with nc.Block() as blk:
                    @blk.tensor
                    def _(t, m=m):
                        ac.wait(t)
                        ve.wait(t)
                        bank = pb[4 + m % 2][:, :]
                        for k in range(DC):
                            t.matmul(bank, w2s[:, k, 128 * m:128 * (m + 1)], y1b[:, k, :],
                                     start=(k == 0), stop=False)
                        pe.inc(t.matmul(bank, w2brow[:, 128 * m:128 * (m + 1)],
                                        self.ones512[:], start=False, stop=True))

                    @blk.vector
                    def _(v, m=m):
                        pe.wait(v)
                        bank = pb[4 + m % 2][:, :]
                        self.vs(v, v.tensor_add(yb[:, m, :], bank, outb[:, m, :]))

            for tt in range(4):
                gi = kc * 4 + tt
                with nc.Block() as blk:
                    @blk.gpsimd
                    def _(g_, kc=kc, tt=tt, gi=gi):
                        if gi > 0:
                            g_.wait_ge(ve.sem, acnt[gi - 1])
                            g_.wait_ge(sgd.sem, gcnt[gi - 1])
                            for e_ in range(NE):
                                sgd.inc(g_.dma_gather(
                                    xadd[:, e_:e_ + 1, :], self.moebuf[e_],
                                    self.inv16[:, e_, 8 * gi:8 * gi + 8],
                                    128, r128["r"], D, elem_step=D), 16)
                            gcnt[gi] = sgd.n

                    @blk.scalar
                    def _(s, kc=kc, tt=tt, gi=gi):
                        ve.wait(s)
                        lda.inc(s.dma_start(
                            xt2[:, gi % 2, :],
                            din["x_full"][self.row0 + gi * 128:
                                          self.row0 + (gi + 1) * 128, :]), 16)

                    @blk.tensor
                    def _(t, kc=kc, tt=tt):
                        ve.wait(t)
                        for k in range(DC):
                            ins = t.transpose(psb[:, 128 * k:128 * (k + 1)],
                                              yb[:, k, 128 * tt:128 * (tt + 1)],
                                              self.identb[:])
                            if k == DC - 1:
                                pe.inc(ins)

                    @blk.vector
                    def _(v, kc=kc, par=par, tt=tt):
                        pe.wait(v)
                        self.vs(v, v.bn_stats(self.mv[:, 36, 0:6], psb[:, 0:512]))
                        self.vs(v, v.bn_stats(self.mv[:, 36, 6:12], psb[:, 512:1024]))
                        self.vs(v, v.bn_aggr(self.mv[:, 36, 12:14], self.mv[:, 36, 0:12]))

                    @blk.scalar
                    def _(s, tt=tt):
                        ve.wait(s)
                        ac.inc(s.activation(self.mv[:, 36, 14:15], self.mv[:, 36, 13:14],
                                            AF.Sqrt, bias=self.epsc[:, 0:1]))

                    @blk.vector
                    def _(v, kc=kc, par=par, tt=tt, gi=gi):
                        ac.wait(v)
                        if gi >= 2:
                            self.stx[gi % 2].wait(v)
                        v.wait_ge(sgd.sem, gcnt[gi])
                        self.vs(v, v.reciprocal(self.mv[:, 36, 15:16], self.mv[:, 36, 14:15]))
                        self.vs(v, v.tensor_scalar(self.s1[:, 0:1024], psb[:],
                                                   self.mv[:, 36, 12:13], self.mv[:, 36, 15:16],
                                                   OP.subtract, OP.mult))
                        self.vs(v, v.tensor_tensor(self.s1[:, 0:1024], self.s1[:, 0:1024],
                                                   gbc[:], OP.mult))
                        self.vs(v, v.tensor_add(self.s1[:, 0:1024], self.s1[:, 0:1024], bbc[:]))
                        lda.wait(v)
                        self.vs(v, v.tensor_add(x1t[:, gi % 2, :],
                                                self.s1[:, 0:1024], xt2[:, gi % 2, :]))
                        for e_ in range(NE):
                            self.vs(v, v.tensor_add(x1t[:, gi % 2, :], x1t[:, gi % 2, :],
                                                    xadd[:, e_, :]))
                        acnt[gi] = ve.n

                    @blk.sync
                    def _(sy, kc=kc, tt=tt, tok0=tok0):
                        ve.wait(sy)
                        self.stx[(kc * 4 + tt) % 2].inc(sy.dma_start(
                            self.x1_hbm[tok0 + tt * 128:tok0 + (tt + 1) * 128, :],
                            x1t[:, (kc * 4 + tt) % 2, :]), 16)
        with nc.Block() as blk:
            @blk.sync
            def _(sy):
                self.stx[0].wait(sy)
                self.stx[1].wait(sy)
        self.bar()
        R.close()

    def p4_moe(self):
        nc, din = self.nc, self.din
        lsw, pe, ve, ac, gp, sgd, wl = (self.lsw, self.pe, self.ve, self.ac,
                                        self.gp, self.sgd, self.wl)
        R = ExitStack()
        r = R.enter_context
        xgA = r(nc.sbuf_tensor("p4_xgA", [128, DC, 640], BF16))
        xgB = r(nc.sbuf_tensor("p4_xgB", [128, DC, 512], BF16))
        hid = r(nc.sbuf_tensor("p4_hid", [128, FC, CAP], BF16))
        wbuf = r(nc.sbuf_tensor("p4_wbuf", [128, 4, 8, 1024], BF16))
        osrc = r(nc.sbuf_tensor("p4_osrc", [128, 4, 1024], BF16))
        zrow = r(nc.sbuf_tensor("p4_zrow", [1, D], BF16))
        cgA = r(nc.sbuf_tensor("p4_cgA", [1, 1], U32))
        cgB = r(nc.sbuf_tensor("p4_cgB", [1, 1], U32))
        b1e = r(nc.sbuf_tensor("p4_b1e", [1, FF], BF16))
        ones512b = r(nc.sbuf_tensor("p4_ones512b", [1, 512], BF16))
        b2e = r(nc.sbuf_tensor("p4_b2e", [1, D], BF16))
        pb = [r(nc.psum_tensor(f"p4_pb{i}", [128, 512], F32)) for i in range(8)]

        regs = {}
        with nc.Block() as blk:
            @blk.vector
            def _(v):
                self.vs(v, v.memset(xgA[:], 0.0))
                self.vs(v, v.memset(xgB[:], 0.0))
                self.vs(v, v.memset(ones512b[:], 1.0))
                self.vs(v, v.memset(zrow[:], 0.0))
                self.vs(v, v.memset(cgA[:], 640))
                self.vs(v, v.memset(cgB[:], 512))

            @blk.gpsimd
            def _(g):
                g.load_library(LIB_MLP)
                ve.wait(g)
                regs["cA"] = g.alloc_register("r_cA")
                regs["cB"] = g.alloc_register("r_cB")
                g.load(regs["cA"], cgA[0:1, 0:1])
                g.load(regs["cB"], cgB[0:1, 0:1])

            @blk.sync
            def _(sy):
                ve.wait(sy)
                for e_ in range(NE):
                    self.st[1].inc(sy.dma_start(self.moebuf[e_, CAP:CAP + 1, :],
                                                zrow[:]), 16)
        self.bar()

        for ee in range(NE):
            parb = ee % 2
            with nc.Block() as blk:
                @blk.gpsimd
                def _(g, ee=ee):
                    ve.wait(g)
                    ac.wait(g)
                    pe.wait(g)
                    gp.wait(g)
                    lsw.inc(g.dma_start(b1e[:], din["b1"][ee:ee + 1, :]), 16)
                    lsw.inc(g.dma_start(b2e[:], din["b2"][ee:ee + 1, :]), 16)
                    lsw.wait(g)
                    gp.inc(g.nop())
                    g.load_library(LIB_MLP)
                    sgd.inc(g.dma_gather(xgA[:], self.xn_hbm[:], self.lists[:, ee, 0:40],
                                         640, regs["cA"], D, elem_step=D,
                                         transpose=True), 16)
                    sgd.inc(g.dma_gather(xgB[:], self.xn_hbm[:], self.lists[:, ee, 40:72],
                                         512, regs["cB"], D, elem_step=D,
                                         transpose=True), 16)

                @blk.sync
                def _(sy, ee=ee):
                    pe.wait(sy)
                    if ee == 0:
                        self.wcast.wait(sy)
                    for qm in range(4):
                        for k in range(DC):
                            wl[qm].inc(sy.dma_start(
                                wbuf[:, qm, k, :],
                                self.w1bf[ee, k * 128:(k + 1) * 128,
                                          qm * 1024:(qm + 1) * 1024]), 16)

            for m in range(FC):
                qm = m // 8
                with nc.Block() as blk:
                    @blk.tensor
                    def _(t, ee=ee, m=m, qm=qm, parb=parb):
                        wl[qm].wait(t)
                        if m == 0:
                            sgd.wait(t)
                            lsw.wait(t)
                            ve.wait(t)
                        ac.wait(t)
                        stts = ((xgA, 0, 512, 0), (xgA, 512, 128, 512),
                                (xgB, 0, 512, 640))
                        for stt, (xgt, off, w, hb) in enumerate(stts):
                            bsl = pb[stt + 3 * (m % 2)][:, 0:w]
                            for k in range(DC):
                                t.matmul(bsl, wbuf[:, qm, k, 128 * (m % 8):128 * (m % 8 + 1)],
                                         xgt[:, k, off:off + w], start=(k == 0), stop=False)
                            ins = t.matmul(bsl, b1e[:, 128 * m:128 * (m + 1)],
                                           ones512b[:, 0:w], start=False, stop=True)
                            if stt == 2:
                                pe.inc(ins)

                    @blk.scalar
                    def _(s, m=m):
                        pe.wait(s)
                        stts = ((0, 512, 0), (512, 128, 512), (0, 512, 640))
                        for stt, (off, w, hb) in enumerate(stts):
                            ins = s.activation(hid[:, m, hb:hb + w],
                                               pb[stt + 3 * (m % 2)][:, 0:w], AF.Gelu)
                            if stt == 2:
                                ac.inc(ins)

            for scg in range(3):
                nsc = 4 if scg < 2 else 1
                with nc.Block() as blk:
                    if scg == 0:
                        @blk.sync
                        def _(sy, ee=ee):
                            pe.wait(sy)
                            for kq in range(4):
                                for k in range(DC):
                                    wl[kq].inc(sy.dma_start(
                                        wbuf[:, kq, k, :],
                                        self.w2bf[ee,
                                                  kq * 1024 + k * 128:
                                                  kq * 1024 + (k + 1) * 128,
                                                  :]), 16)

                    @blk.tensor
                    def _(t, ee=ee, scg=scg, nsc=nsc, parb=parb):
                        ac.wait(t)
                        ve.wait(t)
                        for kq in range(4):
                            wl[kq].wait(t)
                            for k in range(DC):
                                kabs = kq * 8 + k
                                for sc in range(nsc):
                                    scabs = scg * 4 + sc
                                    for n in range(2):
                                        nsl = slice(n * 512, n * 512 + 512)
                                        t.matmul(pb[2 * sc + n][:, :],
                                                 hid[:, kabs, 128 * scabs:128 * (scabs + 1)],
                                                 wbuf[:, kq, k, nsl],
                                                 start=(kabs == 0), stop=False)
                        for sc in range(nsc):
                            for n in range(2):
                                nsl = slice(n * 512, n * 512 + 512)
                                ins = t.matmul(pb[2 * sc + n][:, :], self.onesrb[:],
                                               b2e[:, nsl],
                                               start=False, stop=True)
                            pe.inc(ins)

                    @blk.scalar
                    def _(s, ee=ee, scg=scg, nsc=nsc):
                        pe.wait(s)
                        self.st[1].wait(s)
                        for sc in range(nsc):
                            for n in range(2):
                                nsl = slice(n * 512, n * 512 + 512)
                                ins = s.activation(osrc[:, sc, nsl], pb[2 * sc + n][:, :],
                                                   AF.Copy,
                                                   scale=self.gv9[:, ee, scg * 4 + sc:scg * 4 + sc + 1])
                                ac.inc(ins)

                    @blk.sync
                    def _(sy, ee=ee, scg=scg, nsc=nsc):
                        ac.wait(sy)
                        for sc in range(nsc):
                            scabs = scg * 4 + sc
                            self.st[1].inc(sy.dma_start(
                                self.moebuf[ee, 128 * scabs:128 * (scabs + 1), :],
                                osrc[:, sc, :]), 16)
        with nc.Block() as blk:
            @blk.gpsimd
            def _(g):
                sgd.wait(g)

            @blk.sync
            def _(sy):
                self.st[1].wait(sy)
        self.bar()
        R.close()

    def p6_layer1(self):
        nc, din = self.nc, self.din
        ld, lda, lsw, pe, ve, ac, st, trl = (self.ld, self.lda, self.lsw, self.pe,
                                             self.ve, self.ac, self.st, self.trl)
        R = ExitStack()
        r = R.enter_context
        caus = r(nc.sbuf_tensor("p6_caus", [128, 128], BF16))
        wat = r(nc.sbuf_tensor("p6_wat", [128, DC, 3 * D], BF16))
        wpr = r(nc.sbuf_tensor("p6_wpr", [128, DC, D], BF16))
        sgnT = r(nc.sbuf_tensor("p6_sgnT", [128, DC, D], BF16))
        sgntm = r(nc.sbuf_tensor("p6_sgntm", [128, D], BF16))
        xnsl = r(nc.sbuf_tensor("p6_xnsl", [128, DC, 512], BF16))
        qfm = r(nc.sbuf_tensor("p6_qfm", [128, DC, 512], BF16))
        kfm = r(nc.sbuf_tensor("p6_kfm", [128, DC, 512], BF16))
        vfm = r(nc.sbuf_tensor("p6_vfm", [128, DC, 512], BF16))
        ktvt = r(nc.sbuf_tensor("p6_ktvt", [128, 2, 128], BF16))
        atb = r(nc.sbuf_tensor("p6_atb", [128, 128], BF16))
        nsb = r(nc.sbuf_tensor("p6_nsb", [128, 128], BF16))
        S_all = r(nc.sbuf_tensor("p6_Sall", [128, 8, 64], BF16))
        z_all = r(nc.sbuf_tensor("p6_zall", [128, 8, 1], BF16))
        S32 = r(nc.sbuf_tensor("p6_S32", [128, 512], F32))
        z32 = r(nc.sbuf_tensor("p6_z32", [128, 8], F32))
        rden = r(nc.sbuf_tensor("p6_rden", [128, 16], F32))
        ytm = r(nc.sbuf_tensor("p6_ytm", [128, 2, 1024], BF16))
        yfm = r(nc.sbuf_tensor("p6_yfm", [128, DC, 512], BF16))
        l1s = r(nc.sbuf_tensor("p6_l1s", [128, DC, 512], F32))
        x1t = r(nc.sbuf_tensor("p6_x1t", [128, 2, 1024], F32))
        otm = r(nc.sbuf_tensor("p6_otm", [128, 2, 1024], F32))
        pb = [r(nc.psum_tensor(f"p6_pb{i}", [128, 512], F32)) for i in range(6)]
        psb = r(nc.psum_tensor("p6_psb", [128, 1024], BF16))

        with nc.Block() as blk:
            @blk.sync
            def _(sy):
                ld.inc(sy.dma_start(caus[:], self.CST["caus_b"]), 16)

            @blk.gpsimd
            def _(g):
                g.load_library(LIB_STD)
                for k in range(DC):
                    lsw.inc(g.dma_start(wat[:, k, :], din["wattn"][k * 128:(k + 1) * 128, :]), 16)
                    lsw.inc(g.dma_start(wpr[:, k, :], din["wproj"][k * 128:(k + 1) * 128, :]), 16)

        for rt in range(DC):
            with nc.Block() as blk:
                @blk.sync
                def _(sy, rt=rt):
                    ac.wait(sy)
                    ld.inc(sy.dma_start(self.s1[:, 0:1024],
                                        din["L1w"][rt * 128:(rt + 1) * 128, :]), 16)

                @blk.scalar
                def _(s, rt=rt):
                    ld.wait(s)
                    ac.inc(s.activation(sgntm[:], self.s1[:, 0:1024], AF.Sign))

                @blk.tensor
                def _(t, rt=rt):
                    ac.wait(t)
                    ve.wait(t)
                    for k in range(DC):
                        ins = t.transpose(psb[:, 128 * k:128 * (k + 1)],
                                          sgntm[:, 128 * k:128 * (k + 1)], self.identb[:])
                        if k == DC - 1:
                            pe.inc(ins)

                @blk.vector
                def _(v, rt=rt):
                    pe.wait(v)
                    for k in range(DC):
                        self.ve.inc(v.tensor_copy(sgnT[:, k, 128 * rt:128 * (rt + 1)],
                                                  psb[:, 128 * k:128 * (k + 1)]))
                    v.wait_ge(ve.sem, ve.n)
        self.bar()

        NSUB = CH // 128
        for kc in range(NCHUNK):
            tok0 = kc * CH
            par = kc % 2
            with nc.Block() as blk:
                @blk.sync
                def _(sy, kc=kc, par=par, tok0=tok0):
                    pe.wait(sy)
                    for k in range(DC):
                        trl[0].inc(sy.dma_start(
                            xnsl[:, k, :],
                            self.xn2_hbm[tok0:tok0 + CH, k * 128:(k + 1) * 128],
                            transpose=True), 16)

            for m in range(24):
                with nc.Block() as blk:
                    @blk.tensor
                    def _(t, kc=kc, par=par, m=m):
                        trl[0].wait(t)
                        lsw.wait(t)
                        ve.wait(t)
                        ac.wait(t)
                        bank = pb[m % 2][:, :]
                        for k in range(DC):
                            ins = t.matmul(bank, wat[:, k, 128 * m:128 * (m + 1)],
                                           xnsl[:, k, :], start=(k == 0),
                                           stop=(k == DC - 1))
                        pe.inc(ins)

                    if m < 16:
                        @blk.vector
                        def _(v, m=m):
                            pe.wait(v)
                            bank = pb[m % 2][:, :]
                            self.vs(v, v.tensor_scalar(self.s1[:, 0:512], bank, 0.0,
                                                       None, OP.min))

                        @blk.scalar
                        def _(s, m=m):
                            ve.wait(s)
                            ac.inc(s.activation(self.s2[:, 0:512], self.s1[:, 0:512], AF.Exp))

                        @blk.vector
                        def _(v, m=m):
                            ac.wait(v)
                            bank = pb[m % 2][:, :]
                            dst = qfm if m < 8 else kfm
                            self.vs(v, v.tensor_scalar(self.s1[:, 0:512], bank, 0.0,
                                                       None, OP.max))
                            self.vs(v, v.tensor_add(dst[:, m % 8, :], self.s1[:, 0:512],
                                                    self.s2[:, 0:512]))
                    else:
                        @blk.scalar
                        def _(s, m=m):
                            pe.wait(s)
                            bank = pb[m % 2][:, :]
                            ac.inc(s.activation(vfm[:, m % 8, :], bank, AF.Copy))

            for c in range(NSUB):
                cglob = (kc % (NCHUNK // 2)) * NSUB + c
                csl = slice(128 * c, 128 * (c + 1))
                with nc.Block() as blk:
                    @blk.vector
                    def _(v, kc=kc, c=c, cglob=cglob):
                        ac.wait(v)
                        pe.wait(v)
                        if cglob > 0:
                            self.vs(v, v.tensor_add(S32[:], S32[:], pb[4][:, :]))
                            self.vs(v, v.tensor_add(z32[:], z32[:], pb[5][:, 0:8]))
                            self.vs(v, v.tensor_copy(
                                S_all.ap().rearrange("p a b -> p (a b)"), S32[:]))
                            self.vs(v, v.tensor_copy(
                                z_all.ap().rearrange("p a b -> p (a b)"), z32[:]))
                        else:
                            self.vs(v, v.memset(S32[:], 0.0))
                            self.vs(v, v.memset(z32[:], 0.0))
                            self.vs(v, v.memset(S_all[:], 0.0))
                            self.vs(v, v.memset(z_all[:], 0.0))

                for h in range(NHEAD):
                    hp = h // 2
                    po = 64 * (h % 2)
                    with nc.Block() as blk:
                        @blk.tensor
                        def _(t, kc=kc, c=c, h=h, hp=hp, po=po, cglob=cglob, csl=csl):
                            ve.wait(t)
                            t.transpose(psb[:, 0:64], kfm[po:po + 64, hp, csl], self.identb[po:po + 64, po:po + 64])
                            pe.inc(t.transpose(psb[:, 64:128], vfm[po:po + 64, hp, csl],
                                               self.identb[po:po + 64, po:po + 64]))

                        @blk.vector
                        def _(v, h=h):
                            pe.wait(v)
                            self.vs(v, v.tensor_copy(ktvt[:, 0, 0:64], psb[:, 0:64]))
                            self.vs(v, v.tensor_copy(ktvt[:, 1, 0:64], psb[:, 64:128]))

                        @blk.tensor
                        def _(t, kc=kc, c=c, h=h, hp=hp, po=po, cglob=cglob, csl=csl):
                            ve.wait(t)
                            pe.inc(t.matmul(pb[2][:, 0:128], kfm[po:po + 64, hp, csl],
                                            qfm[po:po + 64, hp, csl], start=True, stop=True))

                        @blk.vector
                        def _(v, h=h):
                            pe.wait(v)
                            self.vs(v, v.tensor_tensor(atb[:], pb[2][:, 0:128],
                                                       caus[:], OP.mult))

                        @blk.tensor
                        def _(t, kc=kc, c=c, h=h, hp=hp, po=po, cglob=cglob, csl=csl):
                            ve.wait(t)
                            nbank = pb[3][0:64, 0:128]
                            t.matmul(nbank, ktvt[:, 1, 0:64], atb[:], start=True, stop=False)
                            t.matmul(nbank, S_all[po:po + 64, hp, :],
                                     qfm[po:po + 64, hp, csl], start=False, stop=True)
                            dbank = pb[2][:, 256 + h:256 + h + 1]
                            t.matmul(dbank, atb[:], self.onesc[:], start=True, stop=False)
                            pe.inc(t.matmul(dbank, qfm[po:po + 64, hp, csl],
                                            z_all[po:po + 64, hp, :], start=False, stop=True))
                            sbank = pb[4][po:po + 64, 64 * hp:64 * hp + 64]
                            t.matmul(sbank, ktvt[:, 0, 0:64], ktvt[:, 1, 0:64],
                                     start=True, stop=True)
                            zbank = pb[5][po:po + 64, hp:hp + 1]
                            pe.inc(t.matmul(zbank, ktvt[:, 0, 0:64], self.onesc[:],
                                            start=True, stop=True))

                        @blk.vector
                        def _(v, h=h):
                            pe.wait(v)
                            self.vs(v, v.tensor_copy(nsb[0:64, :], pb[3][0:64, 0:128]))

                        @blk.tensor
                        def _(t, h=h):
                            ve.wait(t)
                            pe.inc(t.transpose(psb[:, 128:192], nsb[0:64, :], self.identb[0:64, 0:64]))

                        @blk.vector
                        def _(v, kc=kc, c=c, h=h):
                            pe.wait(v)
                            self.vs(v, v.tensor_copy(self.s1[:, 64 * h:64 * h + 64],
                                                     psb[:, 128:192]))

                with nc.Block() as blk:
                    @blk.vector
                    def _(v, kc=kc, c=c):
                        pe.wait(v)
                        st[2 + c % 2].wait(v)
                        self.vs(v, v.tensor_scalar(rden[:, 0:16], pb[2][:, 256:272],
                                                   float(EPS_ATT), None, OP.add))
                        self.vs(v, v.reciprocal(rden[:, 0:16], rden[:, 0:16]))
                        for h in range(NHEAD):
                            self.vs(v, v.tensor_scalar(ytm[:, c % 2, 64 * h:64 * h + 64],
                                                       self.s1[:, 64 * h:64 * h + 64],
                                                       rden[:, h:h + 1], None, OP.mult))

                    @blk.sync
                    def _(sy, kc=kc, c=c, tok0=tok0):
                        ve.wait(sy)
                        st[2 + c % 2].inc(sy.dma_start(
                            self.yat_hbm[tok0 + 128 * c:tok0 + 128 * (c + 1), :],
                            ytm[:, c % 2, :]), 16)

        with nc.Block() as blk:
            @blk.sync
            def _(sy):
                st[2].wait(sy)
                st[3].wait(sy)
        self.bar()

        for kc in range(NCHUNK):
            tok0 = kc * CH
            par = kc % 2
            with nc.Block() as blk:
                @blk.sync
                def _(sy, kc=kc, par=par, tok0=tok0):
                    pe.wait(sy)
                    for k in range(DC):
                        trl[0].inc(sy.dma_start(
                            yfm[:, k, :],
                            self.yat_hbm[tok0:tok0 + CH, k * 128:(k + 1) * 128],
                            transpose=True), 16)
                        trl[0].inc(sy.dma_start(
                            xnsl[:, k, :],
                            self.xn2_hbm[tok0:tok0 + CH, k * 128:(k + 1) * 128],
                            transpose=True), 16)

            for m in range(DC):
                with nc.Block() as blk:
                    @blk.tensor
                    def _(t, kc=kc, par=par, m=m):
                        trl[0].wait(t)
                        ve.wait(t)
                        bank = pb[m % 2][:, :]
                        for k in range(DC):
                            t.matmul(bank, wpr[:, k, 128 * m:128 * (m + 1)],
                                     yfm[:, k, :], start=(k == 0), stop=False)
                        for k in range(DC):
                            ins = t.matmul(bank, sgnT[:, k, 128 * m:128 * (m + 1)],
                                           xnsl[:, k, :], start=False,
                                           stop=(k == DC - 1))
                        pe.inc(ins)

                    @blk.vector
                    def _(v, m=m):
                        pe.wait(v)
                        bank = pb[m % 2][:, :]
                        self.vs(v, v.tensor_copy(l1s[:, m, :], bank))

            for tt in range(4):
                with nc.Block() as blk:
                    @blk.scalar
                    def _(s, kc=kc, tt=tt, tok0=tok0):
                        ve.wait(s)
                        lda.inc(s.dma_start(
                            x1t[:, tt % 2, :],
                            self.x1_hbm[tok0 + tt * 128:tok0 + (tt + 1) * 128, :]), 16)

                    @blk.tensor
                    def _(t, kc=kc, tt=tt):
                        ve.wait(t)
                        for k in range(DC):
                            pbk = pb[4] if k < 4 else pb[5]
                            ins = t.transpose(pbk[:, 128 * (k % 4):128 * (k % 4 + 1)],
                                              l1s[:, k, 128 * tt:128 * (tt + 1)],
                                              self.ident[:])
                            if k == DC - 1:
                                pe.inc(ins)

                    @blk.vector
                    def _(v, kc=kc, tt=tt):
                        pe.wait(v)
                        lda.wait(v)
                        if kc * 4 + tt >= 2:
                            self.stx[(kc * 4 + tt) % 2].wait(v)
                        self.vs(v, v.tensor_add(otm[:, tt % 2, 0:512],
                                                pb[4][:, :], x1t[:, tt % 2, 0:512]))
                        self.vs(v, v.tensor_add(otm[:, tt % 2, 512:1024],
                                                pb[5][:, :], x1t[:, tt % 2, 512:1024]))

                    @blk.sync
                    def _(sy, kc=kc, tt=tt, tok0=tok0):
                        ve.wait(sy)
                        self.stx[(kc * 4 + tt) % 2].inc(sy.dma_start(
                            self.out_hbm[tok0 + tt * 128:tok0 + (tt + 1) * 128, :],
                            otm[:, tt % 2, :]), 16)
            self.bar()
        with nc.Block() as blk:
            @blk.sync
            def _(sy):
                self.stx[0].wait(sy)
                self.stx[1].wait(sy)
        self.bar()
        R.close()

    def build(self):
        nc = self.nc
        with nc.named_scope("p0"):
            self.p0_loads()
        with nc.named_scope("p1_ln"):
            self.p_ln(True, self.xn_hbm)
        with nc.named_scope("p2_scores"):
            self.p2_scores()
        with nc.named_scope("p4_moe"):
            self.p4_moe()
        with nc.named_scope("p3a_scan"):
            self.p3a_scan()
        with nc.named_scope("p3b_tail"):
            self.p3b_tail()
        with nc.named_scope("p5_ln2"):
            self.p_ln(False, self.xn2_hbm)
        with nc.named_scope("p6_layer1"):
            self.p6_layer1()
        self.S.close()
        return self.nc


def build_program():
    kb = KB()
    nc = kb.build()
    return nc


def kernel(**inputs):
    x = np.ascontiguousarray(inputs["L0_attn_x"], dtype=np.float32).reshape(NTOK_G, D)
    base = {
        "x_full": x,
        "Win": np.ascontiguousarray(inputs["L0_attn_W_in"], np.float32),
        "bin_": np.ascontiguousarray(inputs["L0_attn_b_in"], np.float32),
        "plog": np.ascontiguousarray(inputs["L0_attn_params_log"], np.float32),
        "Wmid": np.ascontiguousarray(inputs["L0_attn_W_mid"], np.float32),
        "bmid": np.ascontiguousarray(inputs["L0_attn_b_mid"], np.float32),
        "w1w": np.ascontiguousarray(inputs["L0_attn_w1_w"], np.float32),
        "w1b": np.ascontiguousarray(inputs["L0_attn_w1_b"], np.float32).reshape(1, D),
        "w2w": np.ascontiguousarray(inputs["L0_attn_w2_w"], np.float32),
        "w2b": np.ascontiguousarray(inputs["L0_attn_w2_b"], np.float32).reshape(1, D),
        "lng": np.ascontiguousarray(inputs["L0_attn_ln_g"], np.float32).reshape(1, D),
        "lnb": np.ascontiguousarray(inputs["L0_attn_ln_b"], np.float32).reshape(1, D),
        "Wg": np.ascontiguousarray(inputs["L0_ffn_Wg"], np.float32),
        "W1": np.ascontiguousarray(inputs["L0_ffn_W1"], np.float32),
        "b1": np.ascontiguousarray(inputs["L0_ffn_b1"], np.float32),
        "W2": np.ascontiguousarray(inputs["L0_ffn_W2"], np.float32),
        "b2": np.ascontiguousarray(inputs["L0_ffn_b2"], np.float32),
        "wattn": np.ascontiguousarray(inputs["L1_attn_w_attn"], np.float32),
        "wproj": np.ascontiguousarray(inputs["L1_attn_w_proj"], np.float32),
        "L1w": np.ascontiguousarray(inputs["L1_ffn_weight"], np.float32),
    }
    nc = build_program()
    lower_extended_insts(nc)
    in_maps = []
    for i in range(NCORES):
        m = dict(base)
        m["x_full"] = np.roll(x, -NTOK * i, axis=0) if i else x
        in_maps.append(m)
    res = run_bass_kernel_spmd(nc, in_maps, core_ids=list(range(NCORES)))
    out = np.concatenate([res.results[i]["out"] for i in range(NCORES)], axis=0)
    return out.reshape(B, T, D)


# version 12
# speedup vs baseline: 1.0131x; 1.0131x over previous
import numpy as np
import ml_dtypes
from contextlib import ExitStack

import concourse.bass as bass
import concourse.mybir as mybir
from concourse.bass_utils import run_bass_kernel_spmd
from concourse.library_config import (sparse_gather as LIB_SG, mlp as LIB_MLP,
                                      standard as LIB_STD)
from concourse.library_overlay import lower_extended_insts

F32 = mybir.dt.float32
BF16 = mybir.dt.bfloat16
I16 = mybir.dt.int16
U32 = mybir.dt.uint32
AF = mybir.ActivationFunctionType
OP = mybir.AluOpType
BF = ml_dtypes.bfloat16

NCORES = 8
B, T, D = 16, 2048, 1024
NTOK_G = B * T
NTOK = NTOK_G // NCORES
DC = D // 128
CH = 512
NB = CH // 128
NCHUNK = NTOK // CH
NE = 8
FF = 4096
FC = FF // 128
CAP = 1152
NHEAD = 16
EPS_LN = 1e-5
EPS_ATT = 1e-6
NSEARCH = 48
SBOUND = 512.0
TWO23 = float(2 ** 23)
PI = float(np.pi)


class Ctr:
    def __init__(self, stack, nc, name):
        self.sem = stack.enter_context(nc.semaphore(name))
        self.n = 0

    def inc(self, ins, k=1):
        ins.then_inc(self.sem, k)
        self.n += k
        return self.n

    def wait(self, eng, n=None):
        eng.wait_ge(self.sem, self.n if n is None else n)


def _consts(nc):
    c = {}

    def add(name, arr):
        c[name] = nc.inline_tensor(np.ascontiguousarray(arr), name=name).ap()

    i128 = np.arange(128, dtype=np.float32)
    add("iota512_f", np.arange(512, dtype=np.float32).reshape(1, 512))
    add("ones_row_f", np.ones((1, 128), np.float32))
    add("ones_row_b", np.ones((1, 128), BF))
    add("ones512_f", np.ones((1, 512), np.float32))
    add("ones_col_b", np.ones((128, 1), BF))
    add("ones_col_f", np.ones((128, 1), np.float32))
    add("ident_f", np.eye(128, dtype=np.float32))
    add("ident_b", np.eye(128, dtype=BF))
    for q in range(NB):
        add(f"upat{q}_b", ((128 * q + i128[:, None]) <= np.arange(512)[None, :]).astype(BF))
    er = np.zeros((8, DC * 512), np.float32)
    for cidx in range(DC):
        er[cidx, cidx * 512:(cidx + 1) * 512] = 1.0
    add("erow_b", er.astype(BF))
    add("caus_b", (i128[:, None] <= i128[None, :]).astype(BF))
    add("iota16_f", (np.arange(16)[:, None] * 256 + np.arange(256)[None, :]).astype(np.float32))
    add("iota2_f", (np.arange(16)[:, None] + 16 * np.arange(256)[None, :]).astype(np.float32))
    i16 = np.arange(16)
    add("tri16s_f", (i16[:, None] < i16[None, :]).astype(np.float32))
    add("tri128s_f", (i128[:, None] < i128[None, :]).astype(np.float32))
    return c


class KB:
    def __init__(self):
        self.nc = bass.Bass("TRN2", target_bir_lowering=False)
        nc = self.nc
        self.CST = _consts(nc)
        self.S = ExitStack()
        self.row0 = 0

        din = {}

        def DI(name, shape):
            din[name] = nc.dram_tensor(name, list(shape), F32, kind="ExternalInput").ap()

        DI("x_full", (NTOK_G, D))
        DI("Win", (3, D, D)); DI("bin_", (3, D)); DI("plog", (3, D))
        DI("Wmid", (2, D, D)); DI("bmid", (2, D))
        DI("w1w", (D, D)); DI("w1b", (1, D)); DI("w2w", (D, D)); DI("w2b", (1, D))
        DI("lng", (1, D)); DI("lnb", (1, D))
        DI("Wg", (D, NE)); DI("W1", (NE, D, FF)); DI("b1", (NE, FF))
        DI("W2", (NE, FF, D)); DI("b2", (NE, D))
        DI("wattn", (D, 3 * D)); DI("wproj", (D, D)); DI("L1w", (D, D))
        self.din = din

        self.out_hbm = nc.dram_tensor("out", [NTOK, D], F32, kind="ExternalOutput").ap()
        self.xbf = nc.dram_tensor("xbf", [NTOK_G, D], BF16, kind="Internal").ap()
        self.xn_hbm = nc.dram_tensor("xn_hbm", [NTOK, D], BF16, kind="Internal").ap()
        self.x1_hbm = nc.dram_tensor("x1_hbm", [NTOK, D], F32, kind="Internal").ap()
        self.sc_hbm = nc.dram_tensor("sc_hbm", [16, 10, 2048], F32, kind="Internal").ap()
        self.ho_hbm = nc.dram_tensor("ho_hbm", [3, DC, 128, NTOK], BF16, kind="Internal").ap()
        self.xn2_hbm = nc.dram_tensor("xn2_hbm", [NTOK, D], BF16, kind="Internal").ap()
        self.yat_hbm = nc.dram_tensor("yat_hbm", [NTOK, D], BF16, kind="Internal").ap()
        self.moebuf = nc.dram_tensor("moebuf", [NE, CAP + 1, D], BF16,
                                     kind="Internal").ap()
        self.w1bf = nc.dram_tensor("w1bf", [NE, D, FF], BF16, kind="Internal").ap()
        self.w2bf = nc.dram_tensor("w2bf", [NE, FF, D], BF16, kind="Internal").ap()

        e = self.S.enter_context
        self.onesr = e(nc.sbuf_tensor("onesr", [1, 128], F32))
        self.onesrb = e(nc.sbuf_tensor("onesrb", [1, 128], BF16))
        self.ones512 = e(nc.sbuf_tensor("ones512", [1, 512], F32))
        self.onesc = e(nc.sbuf_tensor("onesc", [128, 1], BF16))
        self.onescf = e(nc.sbuf_tensor("onescf", [128, 1], F32))
        self.ident = e(nc.sbuf_tensor("ident", [128, 128], F32))
        self.identb = e(nc.sbuf_tensor("identb", [128, 128], BF16))
        self.cw = e(nc.sbuf_tensor("cw", [1, 16], F32))
        self.wg9 = e(nc.sbuf_tensor("wg9", [128, DC, 9], BF16))
        self.cnt8 = e(nc.sbuf_tensor("cnt8", [128, 8], F32))
        self.lo8 = e(nc.sbuf_tensor("lo8", [1, 8], F32))
        self.hi8 = e(nc.sbuf_tensor("hi8", [1, 8], F32))
        self.mid8 = e(nc.sbuf_tensor("mid8", [1, 8], F32))
        self.ge8 = e(nc.sbuf_tensor("ge8", [1, 8], F32))
        self.t8 = e(nc.sbuf_tensor("t8", [1, 8], F32))
        self.u8 = e(nc.sbuf_tensor("u8", [1, 8], F32))
        self.th = e(nc.sbuf_tensor("th", [128, 8], F32))
        self.cwbc = e(nc.sbuf_tensor("cwbc", [128, 8], F32))
        self.lists = e(nc.sbuf_tensor("lists_", [128, NE, 256], I16))
        self.nf = e(nc.sbuf_tensor("nf_", [1, NE], U32))
        self.nf2 = e(nc.sbuf_tensor("nf2_", [1, NE], U32))
        self.gv9 = e(nc.sbuf_tensor("gv9", [128, NE, 9], F32))
        self.inv16 = e(nc.sbuf_tensor("inv16", [128, NE, 256], I16))
        self.mv = e(nc.sbuf_tensor("mv", [128, 37, 16], F32))
        self.epsc = e(nc.sbuf_tensor("epsc", [128, 1], F32))
        self.s1 = e(nc.sbuf_tensor("s1", [128, 2048], F32))
        self.s2 = e(nc.sbuf_tensor("s2", [128, 2048], F32))

        self.ld = Ctr(self.S, nc, "c_ld")
        self.lda = Ctr(self.S, nc, "c_lda")
        self.lsw = Ctr(self.S, nc, "c_lsw")
        self.lxb = Ctr(self.S, nc, "c_lxb")
        self.pe = Ctr(self.S, nc, "c_pe")
        self.ve = Ctr(self.S, nc, "c_ve")
        self.ac = Ctr(self.S, nc, "c_ac")
        self.gp = Ctr(self.S, nc, "c_gp")
        self.sgd = Ctr(self.S, nc, "c_sgd")
        self.wcast = Ctr(self.S, nc, "c_wcast")
        self.wl = [Ctr(self.S, nc, f"c_wl{i}") for i in range(4)]
        self.st = [Ctr(self.S, nc, f"c_st{i}") for i in range(4)]
        self.stx = [Ctr(self.S, nc, "c_stx0"), Ctr(self.S, nc, "c_stx1")]
        self.trl = [Ctr(self.S, nc, f"c_trl{i}") for i in range(4)]

    def vs(self, v, ins):
        self.ve.inc(ins)
        self.ve.wait(v)

    def bar(self):
        self.nc.all_engine_barrier()

    def p0_loads(self):
        nc, din, CST = self.nc, self.din, self.CST
        ld, lsw, ve, ac = self.ld, self.lsw, self.ve, self.ac
        with nc.Block() as blk:
            @blk.sync
            def _(sy):
                for name, sb in [("ones_row_f", self.onesr), ("ones_row_b", self.onesrb),
                                 ("ones512_f", self.ones512),
                                 ("ones_col_b", self.onesc), ("ones_col_f", self.onescf),
                                 ("ident_f", self.ident), ("ident_b", self.identb)]:
                    ld.inc(sy.dma_start(sb[:], CST[name]), 16)

            @blk.gpsimd
            def _(g):
                for k in range(DC):
                    lsw.inc(g.dma_start(self.wg9[:, k, 0:8],
                                        din["Wg"][k * 128:(k + 1) * 128, :]), 16)
                for i in range(8):
                    sl = slice(i * (NTOK_G // 8), (i + 1) * (NTOK_G // 8))
                    self.lxb.inc(g.dma_start(self.xbf[sl, :], din["x_full"][sl, :]), 16)
                for e_ in range(NE):
                    self.wcast.inc(g.dma_start(self.w1bf[e_], din["W1"][e_]), 16)
                    self.wcast.inc(g.dma_start(self.w2bf[e_], din["W2"][e_]), 16)

            @blk.vector
            def _(v):
                ld.wait(v)
                v.wait_ge(lsw.sem, 16 * DC)
                for k in range(DC):
                    ve.inc(v.memset(self.wg9[:, k, 8:9], 1.0))
                ve.inc(v.memset(self.epsc[:], float(EPS_LN)))
                v.wait_ge(ve.sem, ve.n)
        self.bar()

    def p_ln(self, src_is_x, dst_hbm):
        nc = self.nc
        tag = "a" if src_is_x else "b"
        ld, ve, ac, st = self.ld, self.ve, self.ac, self.st
        R = ExitStack()
        r = R.enter_context
        xtm = r(nc.sbuf_tensor(f"ln_xtm_{tag}", [128, 32, D], F32))
        xnb = r(nc.sbuf_tensor(f"ln_xnb_{tag}", [128, 4, D], BF16))
        with nc.Block() as blk:
            @blk.sync
            def _(sy):
                for i in range(32):
                    if src_is_x:
                        src = self.din["x_full"][self.row0 + i * 128: self.row0 + (i + 1) * 128, :]
                    else:
                        src = self.x1_hbm[i * 128:(i + 1) * 128, :]
                    ld.inc(sy.dma_start(xtm[:, i, :], src), 16)

            @blk.vector
            def _(v):
                ld.wait(v)
                for i in range(32):
                    self.vs(v, v.bn_stats(self.mv[:, i, 0:6], xtm[:, i, 0:512]))
                    self.vs(v, v.bn_stats(self.mv[:, i, 6:12], xtm[:, i, 512:1024]))
                    self.vs(v, v.bn_aggr(self.mv[:, i, 12:14], self.mv[:, i, 0:12]))

            @blk.scalar
            def _(s):
                ve.wait(s)
                ac.inc(s.activation(self.mv[:, 0:32, 14:15], self.mv[:, 0:32, 13:14],
                                    AF.Sqrt, bias=self.epsc[:, 0:1]))
        st_base = [c.n for c in st]
        with nc.Block() as blk:
            ve_marks = []

            @blk.vector
            def _(v):
                ac.wait(v)
                self.vs(v, v.reciprocal(self.mv[:, 0:32, 15:16], self.mv[:, 0:32, 14:15]))
                for i in range(32):
                    if i >= 4:
                        st[i % 4].wait(v, st_base[i % 4] + 16 * (i // 4))
                    self.vs(v, v.tensor_scalar(xnb[:, i % 4, :], xtm[:, i, :],
                                               self.mv[:, i, 12:13], self.mv[:, i, 15:16],
                                               OP.subtract, OP.mult))
                    ve_marks.append(self.ve.n)

            @blk.sync
            def _(sy):
                for i in range(32):
                    sy.wait_ge(ve.sem, ve_marks[i])
                    st[i % 4].inc(sy.dma_start(dst_hbm[i * 128:(i + 1) * 128, :],
                                               xnb[:, i % 4, :]), 16)
        self.bar()
        with nc.Block() as blk:
            @blk.sync
            def _(sy):
                for j in range(4):
                    st[j].wait(sy)
        self.bar()
        R.close()

    def p2_scores(self):
        nc, din = self.nc, self.din
        ld, lsw, pe, ve, ac, gp, st, trl = (self.ld, self.lsw, self.pe, self.ve,
                                            self.ac, self.gp, self.st, self.trl)
        sgd = self.sgd
        R = ExitStack()
        r = R.enter_context
        xfm = r(nc.sbuf_tensor("p2_xfm", [128, 4, 2048], BF16))
        x2f = r(nc.sbuf_tensor("p2_x2f", [128, 4, 2048], BF16))
        sc10 = r(nc.sbuf_tensor("p2_sc10", [9, 2048], F32))
        ssq = r(nc.sbuf_tensor("p2_ssq", [1, 2048], F32))
        self.scores = r(nc.sbuf_tensor("scores", [128, NE * 256], F32))
        self.mu = r(nc.sbuf_tensor("mu_", [128, 256], F32))
        self.sq = r(nc.sbuf_tensor("sq_", [128, 256], F32))
        self.rstd = r(nc.sbuf_tensor("rstd_", [128, 256], F32))
        self.srch = r(nc.sbuf_tensor("srch", [128, NE * 256], BF16))
        self.gmy = r(nc.sbuf_tensor("gmy", [16, 256], F32))
        self.mif = r(nc.sbuf_tensor("mif", [16, 256], F32))
        self.m01 = r(nc.sbuf_tensor("m01", [16, 256], F32))
        self.mii = r(nc.sbuf_tensor("mii", [16, 256], I16))
        self.iota16 = r(nc.sbuf_tensor("iota16", [16, 256], F32))
        glist = r(nc.sbuf_tensor("p2_glist", [16, 256], F32))
        pb = [r(nc.psum_tensor(f"p2_pb{i}", [128, 512], F32)) for i in range(8)]

        with nc.Block() as blk:
            @blk.sync
            def _(sy):
                ld.inc(sy.dma_start(self.iota16[:], self.CST["iota2_f"]), 16)

            @blk.tensor
            def _(t):
                self.lsw.wait(t, 16 * DC)
                ve.wait(t)
                for k in range(DC):
                    pe.inc(t.matmul(pb[0][0:1, 0:9], self.onesc[:], self.wg9[:, k, :],
                                    start=(k == 0), stop=(k == DC - 1)))

            @blk.vector
            def _(v):
                pe.wait(v)
                self.vs(v, v.tensor_copy(self.cw[:, 0:9], pb[0][0:1, 0:9]))
        self.bar()

        pes = []
        for i in range(16):
            for k in range(DC):
                g_ = i * DC + k
                par = g_ % 4
                with nc.Block() as blk:
                    @blk.sync
                    def _(sy, i=i, k=k, par=par, g_=g_):
                        if g_ >= 4:
                            sy.wait_ge(pe.sem, pes[g_ - 4])
                        self.lxb.wait(sy)
                        trl[par].inc(sy.dma_start(
                            xfm[:, par, :],
                            self.xbf[i * 2048:(i + 1) * 2048, k * 128:(k + 1) * 128],
                            transpose=True), 16)

                    @blk.scalar
                    def _(s, i=i, k=k, par=par, g_=g_):
                        if g_ >= 4:
                            s.wait_ge(pe.sem, pes[g_ - 4])
                        trl[par].wait(s)
                        ac.inc(s.activation(x2f[:, par, :], xfm[:, par, :], AF.Square))

                    @blk.tensor
                    def _(t, i=i, k=k, par=par):
                        ac.wait(t)
                        trl[par].wait(t)
                        if k == 0:
                            ve.wait(t)
                        for n in range(4):
                            sl = slice(n * 512, (n + 1) * 512)
                            t.matmul(pb[n][0:9, :], self.wg9[:, k, :],
                                     xfm[:, par, sl], start=(k == 0), stop=(k == DC - 1))
                            pe.inc(t.matmul(pb[4 + n][0:1, :], self.onesc[:],
                                            x2f[:, par, sl], start=(k == 0),
                                            stop=(k == DC - 1)))
                        pes.append(pe.n)
            with nc.Block() as blk:
                @blk.vector
                def _(v, i=i):
                    pe.wait(v)
                    if i >= 1:
                        st[2].wait(v)
                    for n in range(4):
                        sl = slice(n * 512, (n + 1) * 512)
                        self.vs(v, v.tensor_copy(sc10[:, sl], pb[n][0:9, :]))
                        self.vs(v, v.tensor_copy(ssq[:, sl], pb[4 + n][0:1, :]))

                @blk.sync
                def _(sy, i=i):
                    ve.wait(sy)
                    st[2].inc(sy.dma_start(self.sc_hbm[i, 0:9, :], sc10[:]), 16)
                    st[2].inc(sy.dma_start(self.sc_hbm[i, 9:10, :], ssq[:]), 16)
        self.bar()
        with nc.Block() as blk:
            @blk.sync
            def _(sy):
                st[2].wait(sy)
        self.bar()

        with nc.Block() as blk:
            @blk.sync
            def _(sy):
                sc4 = self.sc_hbm.rearrange("i r (jb f) -> i r jb f", jb=8)
                for a in range(16):
                    ld.inc(sy.dma_start(
                        self.scores[8 * a:8 * a + 8, :].rearrange("p (e f) -> p e f", e=8),
                        sc4[a, 0:8, :, :].rearrange("e b f -> b e f")), 16)
                    ld.inc(sy.dma_start(self.mu[8 * a:8 * a + 8, :], sc4[a, 8, :, :]), 16)
                    ld.inc(sy.dma_start(self.sq[8 * a:8 * a + 8, :], sc4[a, 9, :, :]), 16)

            @blk.vector
            def _(v):
                ld.wait(v)
                self.vs(v, v.tensor_scalar(self.mu[:], self.mu[:], 1.0 / D, None, OP.mult))
                self.vs(v, v.tensor_tensor(self.rstd[:], self.mu[:], self.mu[:], OP.mult))
                self.vs(v, v.scalar_tensor_tensor(self.rstd[:], self.sq[:], 1.0 / D,
                                                  self.rstd[:], OP.mult, OP.subtract))

            @blk.scalar
            def _(s):
                ve.wait(s)
                ac.inc(s.activation(self.rstd[:], self.rstd[:], AF.Sqrt, bias=self.epsc[:, 0:1]))

            @blk.vector
            def _(v):
                ac.wait(v)
                self.vs(v, v.reciprocal(self.rstd[:], self.rstd[:]))

            @blk.tensor
            def _(t):
                ve.wait(t)
                pe.inc(t.matmul(pb[0][0:128, 0:8], self.onesr[:], self.cw[:, 0:8],
                                start=True, stop=True))

            @blk.vector
            def _(v):
                pe.wait(v)
                self.vs(v, v.tensor_copy(self.cwbc[:], pb[0][0:128, 0:8]))
                for e_ in range(NE):
                    sl = slice(e_ * 256, (e_ + 1) * 256)
                    self.vs(v, v.tensor_scalar(self.s1[:, 0:256], self.mu[:],
                                               self.cwbc[:, e_:e_ + 1], None, OP.mult))
                    self.vs(v, v.tensor_sub(self.s1[:, 0:256], self.scores[:, sl],
                                            self.s1[:, 0:256]))
                    self.vs(v, v.tensor_tensor(self.scores[:, sl], self.s1[:, 0:256],
                                               self.rstd[:], OP.mult))

        self.bar()

        with nc.Block() as blk:
            @blk.vector
            def _(v):
                self.vs(v, v.memset(self.lo8[:], -SBOUND))
                self.vs(v, v.memset(self.hi8[:], SBOUND))
        self.bar()
        for it in range(NSEARCH):
            with nc.Block() as blk:
                @blk.vector
                def _(v):
                    self.vs(v, v.tensor_add(self.t8[:], self.lo8[:], self.hi8[:]))
                    self.vs(v, v.tensor_scalar(self.mid8[:], self.t8[:], 0.5, None, OP.mult))

                @blk.tensor
                def _(t):
                    ve.wait(t)
                    midv = self.mid8[0:1, :].rearrange("p e -> p e ()").to_broadcast([1, 8, 256])
                    for n in range(4):
                        pe.inc(t.matmul(pb[n][:, :], self.onesr[:],
                                        midv[0:1, 2 * n:2 * n + 2, :],
                                        start=True, stop=True))

                @blk.vector
                def _(v):
                    pe.wait(v)
                    for n in range(4):
                        sl = slice(n * 512, (n + 1) * 512)
                        self.vs(v, v.tensor_tensor(self.srch[:, sl], self.scores[:, sl],
                                                   pb[n][:, :], OP.is_ge))
                    for e_ in range(NE):
                        self.vs(v, v.reduce_sum(self.cnt8[:, e_:e_ + 1],
                                                self.srch[:, e_ * 256:(e_ + 1) * 256],
                                                mybir.AxisListType.X))

                @blk.tensor
                def _(t):
                    ve.wait(t)
                    pe.inc(t.matmul(pb[4][0:1, 0:8], self.onescf[:], self.cnt8[:],
                                    start=True, stop=True))

                @blk.vector
                def _(v):
                    pe.wait(v)
                    self.vs(v, v.tensor_scalar(self.ge8[:], pb[4][0:1, 0:8],
                                               float(NTOK_G // 4), None, OP.is_ge))
                    self.vs(v, v.tensor_sub(self.t8[:], self.mid8[:], self.lo8[:]))
                    self.vs(v, v.tensor_tensor(self.t8[:], self.t8[:], self.ge8[:], OP.mult))
                    self.vs(v, v.tensor_add(self.lo8[:], self.lo8[:], self.t8[:]))
                    self.vs(v, v.tensor_sub(self.u8[:], self.hi8[:], self.mid8[:]))
                    self.vs(v, v.tensor_scalar(self.t8[:], self.ge8[:], -1.0, 1.0,
                                               OP.mult, OP.add))
                    self.vs(v, v.tensor_tensor(self.u8[:], self.u8[:], self.t8[:], OP.mult))
                    self.vs(v, v.tensor_sub(self.hi8[:], self.hi8[:], self.u8[:]))
        self.bar()
        with nc.Block() as blk:
            @blk.tensor
            def _(t):
                ve.wait(t)
                pe.inc(t.matmul(pb[0][0:128, 0:8], self.onesr[:], self.lo8[:],
                                start=True, stop=True))

            @blk.vector
            def _(v):
                pe.wait(v)
                self.vs(v, v.tensor_copy(self.th[:], pb[0][0:128, 0:8]))
        self.bar()

        sc2 = r(nc.sbuf_tensor("p2_sc2", [16, NE, 256], F32))
        mu2 = r(nc.sbuf_tensor("p2_mu2", [16, 256], F32))
        sq2 = r(nc.sbuf_tensor("p2_sq2", [16, 256], F32))
        rstd2 = r(nc.sbuf_tensor("p2_rstd2", [16, 256], F32))
        mask2 = r(nc.sbuf_tensor("p2_mask2", [16, 256], F32))
        tri16 = r(nc.sbuf_tensor("p2_tri16", [16, 16], F32))
        tri128 = r(nc.sbuf_tensor("p2_tri128", [128, 128], F32))
        colrow = r(nc.sbuf_tensor("p2_colrow", [1, 256], F32))
        csT = r(nc.sbuf_tensor("p2_csT", [128, 2], F32))
        excT = r(nc.sbuf_tensor("p2_excT", [128, 2], F32))
        tot2 = r(nc.sbuf_tensor("p2_tot2", [1, 2], F32))
        excrow = r(nc.sbuf_tensor("p2_excrow", [1, 256], F32))
        invf = r(nc.sbuf_tensor("p2_invf", [16, 256], F32))
        sgrid = r(nc.sbuf_tensor("p2_sgrid", [128, 10, 16], F32))
        with nc.Block() as blk:
            @blk.sync
            def _(sy):
                ld.inc(sy.dma_start(tri16[:], self.CST["tri16s_f"]), 16)
                ld.inc(sy.dma_start(tri128[:], self.CST["tri128s_f"]), 16)
        for i in range(2):
            half = slice(128 * i, 128 * (i + 1))
            with nc.Block() as blk:
                @blk.sync
                def _(sy, i=i):
                    ve.wait(sy)
                    ld.inc(sy.dma_start(
                        sgrid[:], self.sc_hbm[i].rearrange("r (b a) -> b r a", a=16)), 16)

                @blk.tensor
                def _(t, i=i):
                    ld.wait(t)
                    ve.wait(t)
                    for rr in range(10):
                        ins = t.transpose(
                            pb[rr // 4][0:16, 128 * (rr % 4):128 * (rr % 4) + 128],
                            sgrid[:, rr, :], self.ident[:])
                        if rr == 9:
                            pe.inc(ins)

                @blk.vector
                def _(v, i=i, half=half):
                    pe.wait(v)
                    for e_ in range(NE):
                        self.vs(v, v.tensor_copy(
                            sc2[0:16, e_, half],
                            pb[e_ // 4][0:16, 128 * (e_ % 4):128 * (e_ % 4) + 128]))
                    self.vs(v, v.tensor_copy(mu2[:, half], pb[2][0:16, 0:128]))
                    self.vs(v, v.tensor_copy(sq2[:, half], pb[2][0:16, 128:256]))
        with nc.Block() as blk:
            @blk.vector
            def _(v):
                ld.wait(v)
                self.vs(v, v.tensor_scalar(mu2[:], mu2[:], 1.0 / D, None, OP.mult))
                self.vs(v, v.tensor_tensor(rstd2[:], mu2[:], mu2[:], OP.mult))
                self.vs(v, v.scalar_tensor_tensor(rstd2[:], sq2[:], 1.0 / D,
                                                  rstd2[:], OP.mult, OP.subtract))

            @blk.scalar
            def _(s):
                ve.wait(s)
                ac.inc(s.activation(rstd2[:], rstd2[:], AF.Sqrt, bias=self.epsc[0:16, 0:1]))

            @blk.vector
            def _(v):
                ac.wait(v)
                self.vs(v, v.reciprocal(rstd2[:], rstd2[:]))
                for e_ in range(NE):
                    self.vs(v, v.tensor_scalar(self.s1[0:16, 0:256], mu2[:],
                                               self.cwbc[0:16, e_:e_ + 1], None, OP.mult))
                    self.vs(v, v.tensor_sub(self.s1[0:16, 0:256], sc2[0:16, e_, :],
                                            self.s1[0:16, 0:256]))
                    self.vs(v, v.tensor_tensor(sc2[0:16, e_, :], self.s1[0:16, 0:256],
                                               rstd2[:], OP.mult))
        self.bar()

        with nc.Block() as blk:
            @blk.vector
            def _(v):
                self.vs(v, v.memset(self.lists.ap().rearrange("p a b -> p (a b)"), 0))
        self.bar()
        for e_ in range(NE):
            with nc.Block() as blk:
                @blk.scalar
                def _(s, e_=e_):
                    ve.wait(s)
                    gp.wait(s)
                    st[0].wait(s)
                    ac.inc(s.activation(self.gmy[:], sc2[0:16, e_, :], AF.Sigmoid))

                @blk.vector
                def _(v, e_=e_):
                    gp.wait(v)
                    ac.wait(v)
                    self.vs(v, v.tensor_scalar(self.m01[:], sc2[0:16, e_, :],
                                               self.th[0:16, e_:e_ + 1], None, OP.is_ge))
                    self.vs(v, v.tensor_scalar(self.mif[:], self.iota16[:], 1.0, None, OP.add))
                    self.vs(v, v.tensor_tensor(self.mif[:], self.m01[:], self.mif[:], OP.mult))
                    self.vs(v, v.tensor_scalar(self.mif[:], self.mif[:], 1.0, None, OP.subtract))
                    self.vs(v, v.tensor_copy(self.mii[:], self.mif[:]))
                    self.vs(v, v.tensor_scalar(self.mif[:], self.gmy[:], 1.0, None, OP.add))
                    self.vs(v, v.tensor_tensor(self.mif[:], self.m01[:], self.mif[:], OP.mult))
                    self.vs(v, v.tensor_scalar(self.mif[:], self.mif[:], 1.0, None, OP.subtract))

                @blk.gpsimd
                def _(g, e_=e_):
                    if e_ == 0:
                        g.load_library(LIB_SG)
                    ve.wait(g)
                    gp.inc(g.sparse_gather(self.lists[0:16, e_, :], self.mii[:],
                                           num_found=self.nf[0:1, e_:e_ + 1]))
                    gp.inc(g.sparse_gather(glist[:], self.mif[:],
                                           num_found=self.nf2[0:1, e_:e_ + 1]))

                @blk.sync
                def _(sy, e_=e_):
                    gp.wait(sy)
                    for pp in range(8):
                        st[0].inc(sy.dma_start(
                            self.gv9[16 * pp:16 * pp + 16, e_, :],
                            glist[:, pp:pp + 72:8][:, 0:9],
                            allow_slow_non_contiguous=True), 16)

            with nc.Block() as blk:
                @blk.vector
                def _(v, e_=e_):
                    gp.wait(v)
                    self.vs(v, v.tensor_copy(invf[:], self.lists[0:16, e_, :]))
                    self.vs(v, v.tensor_scalar(invf[:], invf[:], 0.0, None, OP.max))
                    self.vs(v, v.tensor_copy(self.lists[0:16, e_, :], invf[:]))
                    self.vs(v, v.tensor_copy(mask2[:], self.m01[:]))

                @blk.tensor
                def _(t, e_=e_):
                    ve.wait(t)
                    pe.inc(t.matmul(pb[2][0:1, 0:256], self.onescf[0:16, 0:1],
                                    mask2[:], start=True, stop=True))

                @blk.vector
                def _(v, e_=e_):
                    pe.wait(v)
                    self.vs(v, v.tensor_copy(colrow[:], pb[2][0:1, 0:256]))

                @blk.tensor
                def _(t, e_=e_):
                    ve.wait(t)
                    t.transpose(pb[3][0:128, 0:1], colrow[0:1, 0:128],
                                self.ident[0:1, 0:1])
                    pe.inc(t.transpose(pb[3][0:128, 1:2], colrow[0:1, 128:256],
                                       self.ident[0:1, 0:1]))

                @blk.vector
                def _(v, e_=e_):
                    pe.wait(v)
                    self.vs(v, v.tensor_copy(csT[:], pb[3][0:128, 0:2]))

                @blk.tensor
                def _(t, e_=e_):
                    ve.wait(t)
                    pe.inc(t.matmul(pb[4][0:128, 0:2], tri128[:], csT[:],
                                    start=True, stop=True))
                    pe.inc(t.matmul(pb[5][0:1, 0:2], self.onescf[:], csT[:],
                                    start=True, stop=True))

                @blk.vector
                def _(v, e_=e_):
                    pe.wait(v)
                    self.vs(v, v.tensor_copy(excT[:], pb[4][0:128, 0:2]))
                    self.vs(v, v.tensor_copy(tot2[:], pb[5][0:1, 0:2]))

                @blk.tensor
                def _(t, e_=e_):
                    ve.wait(t)
                    pe.inc(t.matmul(pb[6][0:128, 0:1], self.onesr[:], tot2[0:1, 0:1],
                                    start=True, stop=True))

                @blk.vector
                def _(v, e_=e_):
                    pe.wait(v)
                    self.vs(v, v.tensor_add(excT[:, 1:2], excT[:, 1:2], pb[6][0:128, 0:1]))

                @blk.tensor
                def _(t, e_=e_):
                    ve.wait(t)
                    t.transpose(pb[7][0:1, 0:128], excT[:, 0:1], self.ident[:])
                    pe.inc(t.transpose(pb[7][0:1, 128:256], excT[:, 1:2], self.ident[:]))

                @blk.vector
                def _(v, e_=e_):
                    pe.wait(v)
                    self.vs(v, v.tensor_copy(excrow[:], pb[7][0:1, 0:256]))

                @blk.tensor
                def _(t, e_=e_):
                    ve.wait(t)
                    t.matmul(pb[0][0:16, 0:256], tri16[:], mask2[:],
                             start=True, stop=False)
                    pe.inc(t.matmul(pb[0][0:16, 0:256], self.onesr[0:1, 0:16],
                                    excrow[:], start=False, stop=True))

                @blk.vector
                def _(v, e_=e_):
                    pe.wait(v)
                    self.vs(v, v.tensor_scalar(invf[:], pb[0][0:16, 0:256],
                                               -float(CAP), None, OP.add))
                    self.vs(v, v.tensor_tensor(invf[:], mask2[:], invf[:], OP.mult))
                    self.vs(v, v.tensor_scalar(invf[:], invf[:], float(CAP), None, OP.add))
                    self.vs(v, v.tensor_scalar(invf[:], invf[:], float(CAP), None, OP.min))
                    self.vs(v, v.tensor_copy(self.inv16[0:16, e_, :], invf[:]))
        self.bar()
        with nc.Block() as blk:
            @blk.vector
            def _(v):
                st[0].wait(v)
                self.vs(v, v.tensor_scalar(self.gv9.ap().rearrange("p a b -> p (a b)"),
                                           self.gv9.ap().rearrange("p a b -> p (a b)"),
                                           0.0, None, OP.max))

            @blk.gpsimd
            def _(g):
                ve.wait(g)
                for gix in range(1, 8):
                    sl = slice(16 * gix, 16 * gix + 16)
                    sgd.inc(g.dma_start(self.lists[sl, :, :], self.lists[0:16, :, :]), 16)
                    sgd.inc(g.dma_start(self.inv16[sl, :, :], self.inv16[0:16, :, :]), 16)
                sgd.wait(g)
        self.bar()
        R.close()

    def p3a_scan(self):
        nc, din = self.nc, self.din
        ld, lda, lsw, pe, ve, ac, st, trl = (self.ld, self.lda, self.lsw, self.pe,
                                             self.ve, self.ac, self.st, self.trl)
        R = ExitStack()
        r = R.enter_context
        wri = r(nc.sbuf_tensor("p3_wri", [128, DC, 2048], BF16))
        wo = r(nc.sbuf_tensor("p3_wo", [128, DC, 1024], BF16))
        tGr = r(nc.sbuf_tensor("p3_tGr", [128, NB, D], BF16))
        tGi = r(nc.sbuf_tensor("p3_tGi", [128, NB, D], BF16))
        tLr = r(nc.sbuf_tensor("p3_tLr", [128, DC, 512], BF16))
        tLi = r(nc.sbuf_tensor("p3_tLi", [128, DC, 512], BF16))
        xnsl = r(nc.sbuf_tensor("p3_xnsl", [128, DC, 512], BF16))
        utr = r(nc.sbuf_tensor("p3_utr", [128, NB, D], BF16))
        uti = r(nc.sbuf_tensor("p3_uti", [128, NB, D], BF16))
        hbuf = r(nc.sbuf_tensor("p3_hbuf", [128, 2, DC, 512], BF16))
        osm = r(nc.sbuf_tensor("p3_osm", [128, 2, 512], BF16))
        wcol = r(nc.sbuf_tensor("p3_wcol", [128, 2, 8], F32))
        prow = r(nc.sbuf_tensor("p3_prow", [1, 3 * D], F32))
        pval = r(nc.sbuf_tensor("p3_pval", [1, 3 * D], F32))
        brow = r(nc.sbuf_tensor("p3_brow", [1, 2 * D], F32))
        iota512 = r(nc.sbuf_tensor("p3_iota512", [1, 512], F32))
        upat = r(nc.sbuf_tensor("p3_upat", [128, NB, 512], BF16))
        erow = r(nc.sbuf_tensor("p3_erow", [8, DC * 512], BF16))
        nu8 = r(nc.sbuf_tensor("p3_nu8", [8, 128], F32))
        th8r = r(nc.sbuf_tensor("p3_th8r", [8, 128], F32))
        lamC8 = r(nc.sbuf_tensor("p3_lamC8", [8, 2, 128], F32))
        w8T = r(nc.sbuf_tensor("p3_w8T", [8, 2, 128], BF16))
        s8a = r(nc.sbuf_tensor("p3_s8a", [8, 128], F32))
        s8b = r(nc.sbuf_tensor("p3_s8b", [8, 128], F32))
        pb = [r(nc.psum_tensor(f"p3_pb{i}", [128, 512], F32)) for i in range(8)]

        with nc.Block() as blk:
            @blk.sync
            def _(sy):
                ld.inc(sy.dma_start(iota512[:], self.CST["iota512_f"]), 16)
                for q in range(NB):
                    ld.inc(sy.dma_start(upat[:, q, :], self.CST[f"upat{q}_b"]), 16)
                ld.inc(sy.dma_start(erow[:], self.CST["erow_b"]), 16)
                for i in range(3):
                    ld.inc(sy.dma_start(prow[:, i * D:(i + 1) * D], din["plog"][i:i + 1, :]), 16)
                for i in range(2):
                    ld.inc(sy.dma_start(brow[:, i * D:(i + 1) * D], din["bin_"][i:i + 1, :]), 16)
                ld.inc(sy.dma_start(nu8[:], din["plog"][0:1, :].rearrange("a (p n) -> (a p) n", p=8)), 16)
                ld.inc(sy.dma_start(th8r[:], din["plog"][1:2, :].rearrange("a (p n) -> (a p) n", p=8)), 16)

            @blk.gpsimd
            def _(g):
                g.load_library(LIB_STD)
                for k in range(DC):
                    lsw.inc(g.dma_start(wri[:, k, 0:1024],
                                        din["Win"][0, k * 128:(k + 1) * 128, :]), 16)
                    lsw.inc(g.dma_start(wri[:, k, 1024:2048],
                                        din["Win"][1, k * 128:(k + 1) * 128, :]), 16)
                    lsw.inc(g.dma_start(wo[:, k, :],
                                        din["Win"][2, k * 128:(k + 1) * 128, :]), 16)

            @blk.scalar
            def _(s):
                ld.wait(s)
                ac.inc(s.activation(pval[:], prow[:], AF.Exp))
        self.bar()

        for q in range(NB):
            with nc.Block() as blk:
                @blk.scalar
                def _(s, q=q):
                    ac.inc(s.activation(self.s1[0:1, 0:D], pval[:, 0:D], AF.Copy,
                                        scale=float(128 * q)))
                    ac.inc(s.activation(self.s1[0:1, 1024:2048], pval[:, D:2 * D],
                                        AF.Copy, scale=float(128 * q)))

                @blk.tensor
                def _(t, q=q):
                    ac.wait(t)
                    ld.wait(t)
                    iot = iota512[0:1, 0:128]
                    for half in range(2):
                        sl = slice(half * 512, half * 512 + 512)
                        t.matmul(pb[half][:, :], iot, pval[:, 0:D][:, sl],
                                 start=True, stop=False)
                        pe.inc(t.matmul(pb[half][:, :], self.onesr[:],
                                        self.s1[0:1, 0:D][:, sl], start=False, stop=True))
                        t.matmul(pb[2 + half][:, :], iot, pval[:, D:2 * D][:, sl],
                                 start=True, stop=False)
                        pe.inc(t.matmul(pb[2 + half][:, :], self.onesr[:],
                                        self.s1[0:1, 1024:2048][:, sl], start=False, stop=True))

                @blk.vector
                def _(v, q=q):
                    pe.wait(v)
                    for half in range(2):
                        hs = slice(half * 512, half * 512 + 512)
                        self.vs(v, v.tensor_scalar(self.s2[:, 0:1024][:, hs], pb[2 + half][:, :],
                                                   1.0 / (2 * PI), TWO23, OP.mult, OP.add))
                        self.vs(v, v.tensor_scalar(self.s2[:, 0:1024][:, hs],
                                                   self.s2[:, 0:1024][:, hs],
                                                   TWO23, None, OP.subtract))
                        self.vs(v, v.scalar_tensor_tensor(self.s2[:, 0:1024][:, hs],
                                                          self.s2[:, 0:1024][:, hs],
                                                          -2 * PI, pb[2 + half][:, :],
                                                          OP.mult, OP.add))
                    self.vs(v, v.tensor_scalar(self.s2[:, 0:1024], self.s2[:, 0:1024],
                                               -PI, PI, OP.max, OP.min))

                @blk.scalar
                def _(s, q=q):
                    ve.wait(s)
                    ac.inc(s.activation(self.s1[:, 0:512], pb[0][:, :], AF.Exp))
                    ac.inc(s.activation(self.s1[:, 512:1024], pb[1][:, :], AF.Exp))
                    ac.inc(s.activation(self.s2[:, 1024:2048], self.s2[:, 0:1024], AF.Sin))
                    ac.inc(s.activation(self.s2[:, 0:1024], self.s2[:, 0:1024], AF.Sin,
                                        scale=0.5))

                @blk.tensor
                def _(t, q=q):
                    ac.wait(t)
                    for half in range(2):
                        sl = slice(half * 512, half * 512 + 512)
                        pe.inc(t.matmul(pb[4 + half][:, :], self.onesr[:],
                                        pval[:, 2 * D:3 * D][:, sl], start=True, stop=True))

                @blk.vector
                def _(v, q=q):
                    pe.wait(v)
                    self.vs(v, v.tensor_tensor(self.s2[:, 0:1024], self.s2[:, 0:1024],
                                               self.s2[:, 0:1024], OP.mult))
                    self.vs(v, v.tensor_scalar(self.s2[:, 0:1024], self.s2[:, 0:1024],
                                               -2.0, 1.0, OP.mult, OP.add))
                    for half in range(2):
                        hs = slice(half * 512, half * 512 + 512)
                        self.vs(v, v.tensor_tensor(self.s1[:, 0:1024][:, hs],
                                                   self.s1[:, 0:1024][:, hs],
                                                   pb[4 + half][:, :], OP.mult))
                    self.vs(v, v.tensor_tensor(tGr[:, q, :], self.s1[:, 0:1024],
                                               self.s2[:, 0:1024], OP.mult))
                    self.vs(v, v.tensor_tensor(self.s1[:, 1024:2048], self.s1[:, 0:1024],
                                               self.s2[:, 1024:2048], OP.mult))
                    self.vs(v, v.tensor_scalar(tGi[:, q, :], self.s1[:, 1024:2048],
                                               -1.0, None, OP.mult))
            self.bar()

        for dc in range(DC):
            with nc.Block() as blk:
                @blk.tensor
                def _(t, dc=dc):
                    nsl = slice(dc * 128, (dc + 1) * 128)
                    pe.inc(t.matmul(pb[0][:, :], pval[0:1, 0:D][:, nsl],
                                    iota512[:], start=True, stop=True))
                    pe.inc(t.matmul(pb[1][:, :], pval[0:1, D:2 * D][:, nsl],
                                    iota512[:], start=True, stop=True))

                @blk.vector
                def _(v, dc=dc):
                    pe.wait(v)
                    self.vs(v, v.tensor_scalar(self.s2[:, 0:512], pb[1][:, :],
                                               1.0 / (2 * PI), TWO23, OP.mult, OP.add))
                    self.vs(v, v.tensor_scalar(self.s2[:, 0:512], self.s2[:, 0:512],
                                               TWO23, None, OP.subtract))
                    self.vs(v, v.scalar_tensor_tensor(self.s2[:, 0:512], self.s2[:, 0:512],
                                                      -2 * PI, pb[1][:, :],
                                                      OP.mult, OP.add))
                    self.vs(v, v.tensor_scalar(self.s2[:, 0:512], self.s2[:, 0:512],
                                               -PI, PI, OP.max, OP.min))

                @blk.scalar
                def _(s, dc=dc):
                    ve.wait(s)
                    ac.inc(s.activation(self.s1[:, 0:512], pb[0][:, :], AF.Exp, scale=-1.0))
                    ac.inc(s.activation(self.s2[:, 512:1024], self.s2[:, 0:512], AF.Sin))
                    ac.inc(s.activation(self.s2[:, 0:512], self.s2[:, 0:512], AF.Sin, scale=0.5))

                @blk.vector
                def _(v, dc=dc):
                    ac.wait(v)
                    self.vs(v, v.tensor_tensor(self.s2[:, 0:512], self.s2[:, 0:512],
                                               self.s2[:, 0:512], OP.mult))
                    self.vs(v, v.tensor_scalar(self.s2[:, 0:512], self.s2[:, 0:512],
                                               -2.0, 1.0, OP.mult, OP.add))
                    self.vs(v, v.tensor_tensor(tLr[:, dc, :], self.s1[:, 0:512],
                                               self.s2[:, 0:512], OP.mult))
                    self.vs(v, v.tensor_tensor(tLi[:, dc, :], self.s1[:, 0:512],
                                               self.s2[:, 512:1024], OP.mult))
            self.bar()

        with nc.Block() as blk:
            @blk.scalar
            def _(s):
                ld.wait(s)
                ac.inc(s.activation(s8a[:], nu8[:], AF.Exp))
                ac.inc(s.activation(s8b[:], th8r[:], AF.Exp))

            @blk.vector
            def _(v):
                ac.wait(v)
                self.vs(v, v.tensor_scalar(s8b[:], s8b[:], float(CH), None, OP.mult))
                self.vs(v, v.tensor_scalar(lamC8[:, 0, :], s8b[:],
                                           1.0 / (2 * PI), TWO23, OP.mult, OP.add))
                self.vs(v, v.tensor_scalar(lamC8[:, 0, :], lamC8[:, 0, :],
                                           TWO23, None, OP.subtract))
                self.vs(v, v.scalar_tensor_tensor(s8b[:], lamC8[:, 0, :],
                                                  -2 * PI, s8b[:], OP.mult, OP.add))
                self.vs(v, v.tensor_scalar(s8b[:], s8b[:], -PI, PI, OP.max, OP.min))

            @blk.scalar
            def _(s):
                ve.wait(s)
                ac.inc(s.activation(lamC8[:, 1, :], s8b[:], AF.Sin))
                ac.inc(s.activation(s8b[:], s8b[:], AF.Sin, scale=0.5))
                ac.inc(s.activation(s8a[:], s8a[:], AF.Copy, scale=float(-CH)))
                s.wait_ge(ac.sem, ac.n)
                ac.inc(s.activation(s8a[:], s8a[:], AF.Exp))

            @blk.vector
            def _(v):
                ac.wait(v)
                self.vs(v, v.tensor_tensor(lamC8[:, 0, :], s8b[:], s8b[:], OP.mult))
                self.vs(v, v.tensor_scalar(lamC8[:, 0, :], lamC8[:, 0, :],
                                           -2.0, 1.0, OP.mult, OP.add))
                self.vs(v, v.tensor_tensor(lamC8[:, 0, :], lamC8[:, 0, :], s8a[:], OP.mult))
                self.vs(v, v.tensor_tensor(lamC8[:, 1, :], lamC8[:, 1, :], s8a[:], OP.mult))
        self.bar()

        for kc in range(NCHUNK):
            tok0 = kc * CH
            par = kc % 2
            seq_start = (kc % (NCHUNK // 2) == 0)
            with nc.Block() as blk:
                @blk.sync
                def _(sy, kc=kc, tok0=tok0):
                    pe.wait(sy)
                    for k in range(DC):
                        trl[0].inc(sy.dma_start(
                            xnsl[:, k, :],
                            self.xn_hbm[tok0:tok0 + CH, k * 128:(k + 1) * 128],
                            transpose=True), 16)

                @blk.vector
                def _(v, seq_start=seq_start):
                    if seq_start:
                        self.vs(v, v.memset(w8T[:], 0.0))

            for q in range(NB):
                with nc.Block() as blk:
                    @blk.tensor
                    def _(t, kc=kc, q=q):
                        trl[0].wait(t)
                        lsw.wait(t)
                        ld.wait(t)
                        ve.wait(t)
                        for k in range(DC):
                            for n in range(4):
                                sl = slice(n * 512, (n + 1) * 512)
                                t.matmul(pb[n][:, :],
                                         xnsl[:, k, 128 * q:128 * (q + 1)],
                                         wri[:, k, sl], start=(k == 0), stop=False)
                        for n in range(4):
                            sl = slice(n * 512, (n + 1) * 512)
                            ins = t.matmul(pb[n][:, :], self.onesr[:],
                                           brow[:, sl], start=False, stop=True)
                        pe.inc(ins)

                    @blk.vector
                    def _(v, kc=kc, q=q):
                        pe.wait(v)
                        for hf in range(2):
                            hs = slice(hf * 512, hf * 512 + 512)
                            ur = pb[hf][:, :]
                            ui = pb[2 + hf][:, :]
                            self.vs(v, v.tensor_tensor(self.s1[:, 0:1024][:, hs],
                                                       tGr[:, q, hs], ur, OP.mult))
                            self.vs(v, v.tensor_tensor(self.s2[:, 0:1024][:, hs],
                                                       tGi[:, q, hs], ui, OP.mult))
                        self.vs(v, v.tensor_sub(utr[:, q, :], self.s1[:, 0:1024],
                                                self.s2[:, 0:1024]))
                        for hf in range(2):
                            hs = slice(hf * 512, hf * 512 + 512)
                            ur = pb[hf][:, :]
                            ui = pb[2 + hf][:, :]
                            self.vs(v, v.tensor_tensor(self.s1[:, 0:1024][:, hs],
                                                       tGr[:, q, hs], ui, OP.mult))
                            self.vs(v, v.tensor_tensor(self.s2[:, 0:1024][:, hs],
                                                       tGi[:, q, hs], ur, OP.mult))
                        self.vs(v, v.tensor_add(uti[:, q, :], self.s1[:, 0:1024],
                                                self.s2[:, 0:1024]))

            for dc in range(DC):
                with nc.Block() as blk:
                    @blk.tensor
                    def _(t, kc=kc, dc=dc):
                        ve.wait(t)
                        for comp, psl in ((0, pb[4][:, :]), (1, pb[5][:, :])):
                            ut = utr if comp == 0 else uti
                            for q in range(NB):
                                t.matmul(psl, ut[:, q, 128 * dc:128 * (dc + 1)],
                                         upat[:, q, :], start=(q == 0), stop=False)
                            ins = t.matmul(psl, w8T[:, comp, :],
                                           erow[:, dc * 512:(dc + 1) * 512],
                                           start=False, stop=True)
                            if comp == 1:
                                pe.inc(ins)

                    @blk.vector
                    def _(v, kc=kc, par=par, dc=dc):
                        pe.wait(v)
                        if dc == 0:
                            st[0].wait(v)
                        cr = pb[4][:, :]
                        ci = pb[5][:, :]
                        self.vs(v, v.tensor_tensor(self.s1[:, 0:512], tLr[:, dc, :], cr, OP.mult))
                        self.vs(v, v.tensor_tensor(self.s2[:, 0:512], tLi[:, dc, :], ci, OP.mult))
                        self.vs(v, v.tensor_sub(hbuf[:, 0, dc, :], self.s1[:, 0:512],
                                                self.s2[:, 0:512]))
                        self.vs(v, v.tensor_tensor(self.s1[:, 0:512], tLr[:, dc, :], ci, OP.mult))
                        self.vs(v, v.tensor_tensor(self.s2[:, 0:512], tLi[:, dc, :], cr, OP.mult))
                        self.vs(v, v.tensor_add(hbuf[:, 1, dc, :], self.s1[:, 0:512],
                                                self.s2[:, 0:512]))
                        self.vs(v, v.tensor_copy(wcol[:, 0, dc:dc + 1], cr[:, 511:512]))
                        self.vs(v, v.tensor_copy(wcol[:, 1, dc:dc + 1], ci[:, 511:512]))

            with nc.Block() as blk:
                @blk.tensor
                def _(t, kc=kc):
                    ve.wait(t)
                    t.transpose(pb[6][0:8, 0:128], wcol[:, 0, :], self.ident[:])
                    pe.inc(t.transpose(pb[6][0:8, 128:256], wcol[:, 1, :], self.ident[:]))

                @blk.vector
                def _(v, kc=kc):
                    pe.wait(v)
                    tr = pb[6][0:8, 0:128]
                    ti = pb[6][0:8, 128:256]
                    self.vs(v, v.tensor_tensor(s8a[:], lamC8[:, 0, :], tr, OP.mult))
                    self.vs(v, v.tensor_tensor(s8b[:], lamC8[:, 1, :], ti, OP.mult))
                    self.vs(v, v.tensor_sub(w8T[:, 0, :], s8a[:], s8b[:]))
                    self.vs(v, v.tensor_tensor(s8a[:], lamC8[:, 0, :], ti, OP.mult))
                    self.vs(v, v.tensor_tensor(s8b[:], lamC8[:, 1, :], tr, OP.mult))
                    self.vs(v, v.tensor_add(w8T[:, 1, :], s8a[:], s8b[:]))

            for m in range(DC):
                with nc.Block() as blk:
                    @blk.tensor
                    def _(t, kc=kc, m=m):
                        ac.wait(t)
                        for k in range(DC):
                            ins = t.matmul(pb[7][:, :],
                                           wo[:, k, 128 * m:128 * (m + 1)],
                                           xnsl[:, k, :], start=(k == 0),
                                           stop=(k == DC - 1))
                        pe.inc(ins)

                    @blk.scalar
                    def _(s, kc=kc, m=m, par=par):
                        pe.wait(s)
                        st[2 + m % 2].wait(s)
                        ac.inc(s.activation(osm[:, m % 2, :], pb[7][:, :], AF.Copy))

                    @blk.sync
                    def _(sy, kc=kc, m=m, tok0=tok0):
                        ac.wait(sy)
                        st[2 + m % 2].inc(sy.dma_start(self.ho_hbm[2, m, :, tok0:tok0 + CH],
                                                       osm[:, m % 2, :]), 16)

            with nc.Block() as blk:
                @blk.sync
                def _(sy, kc=kc, par=par, tok0=tok0):
                    ve.wait(sy)
                    for dc in range(DC):
                        st[0].inc(sy.dma_start(self.ho_hbm[0, dc, :, tok0:tok0 + CH],
                                               hbuf[:, 0, dc, :]), 16)
                        st[0].inc(sy.dma_start(self.ho_hbm[1, dc, :, tok0:tok0 + CH],
                                               hbuf[:, 1, dc, :]), 16)
            self.bar()
        with nc.Block() as blk:
            @blk.sync
            def _(sy):
                for j in range(4):
                    st[j].wait(sy)
        self.bar()
        R.close()

    def p3b_tail(self):
        nc, din = self.nc, self.din
        ld, lda, lsw, pe, ve, ac, st = (self.ld, self.lda, self.lsw, self.pe,
                                        self.ve, self.ac, self.st)
        R = ExitStack()
        r = R.enter_context
        wm = r(nc.sbuf_tensor("p3b_wm", [128, DC, 2048], BF16))
        w1s = r(nc.sbuf_tensor("p3b_w1", [128, DC, 1024], BF16))
        w2s = r(nc.sbuf_tensor("p3b_w2", [128, DC, 1024], BF16))
        hsl = r(nc.sbuf_tensor("p3b_hsl", [128, 2, DC, 512], BF16))
        osl = r(nc.sbuf_tensor("p3b_osl", [128, DC, 512], BF16))
        xt2 = r(nc.sbuf_tensor("p3b_xt2", [128, 2, 1024], F32))
        bmd = r(nc.sbuf_tensor("p3b_bmd", [1, D], F32))
        bm1 = r(nc.sbuf_tensor("p3b_bm1", [1, D], F32))
        w1brow = r(nc.sbuf_tensor("p3b_w1b", [1, D], F32))
        w2brow = r(nc.sbuf_tensor("p3b_w2b", [1, D], F32))
        lngr = r(nc.sbuf_tensor("p3b_lng", [1, D], F32))
        lnbr = r(nc.sbuf_tensor("p3b_lnb", [1, D], F32))
        outb = r(nc.sbuf_tensor("p3b_outb", [128, DC, 512], BF16))
        y1b = r(nc.sbuf_tensor("p3b_y1b", [128, DC, 512], BF16))
        yb = r(nc.sbuf_tensor("p3b_yb", [128, DC, 512], BF16))
        gbc = r(nc.sbuf_tensor("p3b_gbc", [128, 1024], BF16))
        bbc = r(nc.sbuf_tensor("p3b_bbc", [128, 1024], BF16))
        x1t = r(nc.sbuf_tensor("p3b_x1t", [128, 2, 1024], F32))
        xadd = r(nc.sbuf_tensor("p3b_xadd", [128, NE, 1024], BF16))
        c128 = r(nc.sbuf_tensor("p3b_c128", [1, 1], U32))
        pb = [r(nc.psum_tensor(f"p3b_pb{i}", [128, 512], F32)) for i in range(6)]
        psb = r(nc.psum_tensor("p3b_psb", [128, 1024], BF16))
        sgd = self.sgd
        gcnt = {}
        acnt = {}
        r128 = {}

        with nc.Block() as blk:
            @blk.sync
            def _(sy):
                ld.inc(sy.dma_start(bmd[:], din["bmid"][0:1, :]), 16)
                ld.inc(sy.dma_start(bm1[:], din["bmid"][1:2, :]), 16)
                ld.inc(sy.dma_start(w1brow[:], din["w1b"][:]), 16)
                ld.inc(sy.dma_start(w2brow[:], din["w2b"][:]), 16)
                ld.inc(sy.dma_start(lngr[:], din["lng"][:]), 16)
                ld.inc(sy.dma_start(lnbr[:], din["lnb"][:]), 16)

            @blk.gpsimd
            def _(g):
                for k in range(DC):
                    lsw.inc(g.dma_start(wm[:, k, 0:1024],
                                        din["Wmid"][0, k * 128:(k + 1) * 128, :]), 16)
                    lsw.inc(g.dma_start(wm[:, k, 1024:2048],
                                        din["Wmid"][1, k * 128:(k + 1) * 128, :]), 16)
                    lsw.inc(g.dma_start(w1s[:, k, :], din["w1w"][k * 128:(k + 1) * 128, :]), 16)
                    lsw.inc(g.dma_start(w2s[:, k, :], din["w2w"][k * 128:(k + 1) * 128, :]), 16)

            @blk.vector
            def _(v):
                lsw.wait(v)
                ld.wait(v)
                self.vs(v, v.tensor_sub(bmd[:], bmd[:], bm1[:]))
                for k in range(DC):
                    self.ve.inc(v.tensor_scalar(wm[:, k, 1024:2048], wm[:, k, 1024:2048],
                                                -1.0, None, OP.mult))
                v.wait_ge(ve.sem, ve.n)

            @blk.tensor
            def _(t):
                ve.wait(t)
                for half in range(2):
                    sl = slice(half * 512, half * 512 + 512)
                    pe.inc(t.matmul(pb[half][:, :], self.onesr[:],
                                    lngr[:, sl], start=True, stop=True))
                    pe.inc(t.matmul(pb[2 + half][:, :], self.onesr[:],
                                    lnbr[:, sl], start=True, stop=True))

            @blk.vector
            def _(v):
                pe.wait(v)
                for half in range(2):
                    sl = slice(half * 512, half * 512 + 512)
                    self.vs(v, v.tensor_copy(gbc[:, sl], pb[half][:, :]))
                    self.vs(v, v.tensor_copy(bbc[:, sl], pb[2 + half][:, :]))

            @blk.vector
            def _(v):
                self.vs(v, v.memset(c128[:], 128))

            @blk.gpsimd
            def _(g_):
                ve.wait(g_)
                lsw.wait(g_)
                g_.load_library(LIB_MLP)
                r128["r"] = g_.alloc_register("r_c128")
                g_.load(r128["r"], c128[0:1, 0:1])
                for e_ in range(NE):
                    sgd.inc(g_.dma_gather(xadd[:, e_:e_ + 1, :], self.moebuf[e_],
                                          self.inv16[:, e_, 0:8], 128, r128["r"], D,
                                          elem_step=D), 16)
                gcnt[0] = sgd.n
        self.bar()

        for kc in range(NCHUNK):
            tok0 = kc * CH
            par = kc % 2
            with nc.Block() as blk:
                @blk.scalar
                def _(s, kc=kc, par=par, tok0=tok0):
                    pe.wait(s)
                    for dc in range(DC):
                        lda.inc(s.dma_start(hsl[:, 0, dc, :],
                                            self.ho_hbm[0, dc, :, tok0:tok0 + CH]), 16)
                        lda.inc(s.dma_start(hsl[:, 1, dc, :],
                                            self.ho_hbm[1, dc, :, tok0:tok0 + CH]), 16)
                        lda.inc(s.dma_start(osl[:, dc, :],
                                            self.ho_hbm[2, dc, :, tok0:tok0 + CH]), 16)

            for m in range(DC):
                with nc.Block() as blk:
                    @blk.tensor
                    def _(t, kc=kc, par=par, m=m):
                        lda.wait(t)
                        lsw.wait(t)
                        ve.wait(t)
                        bank = pb[m % 2][:, :]
                        for k in range(DC):
                            t.matmul(bank, wm[:, k, 128 * m:128 * (m + 1)],
                                     hsl[:, 0, k, :], start=(k == 0), stop=False)
                            t.matmul(bank, wm[:, k, 1024 + 128 * m:1024 + 128 * (m + 1)],
                                     hsl[:, 1, k, :], start=False, stop=False)
                        pe.inc(t.matmul(bank, bmd[:, 128 * m:128 * (m + 1)],
                                        self.ones512[:], start=False, stop=True))

                    @blk.vector
                    def _(v, kc=kc, par=par, m=m):
                        pe.wait(v)
                        bank = pb[m % 2][:, :]
                        self.vs(v, v.tensor_add(outb[:, m, :], bank, osl[:, m, :]))

            for m in range(DC):
                with nc.Block() as blk:
                    @blk.tensor
                    def _(t, m=m):
                        ve.wait(t)
                        ac.wait(t)
                        bank = pb[2 + m % 2][:, :]
                        for k in range(DC):
                            t.matmul(bank, w1s[:, k, 128 * m:128 * (m + 1)], outb[:, k, :],
                                     start=(k == 0), stop=False)
                        pe.inc(t.matmul(bank, w1brow[:, 128 * m:128 * (m + 1)],
                                        self.ones512[:], start=False, stop=True))

                    @blk.scalar
                    def _(s, m=m):
                        pe.wait(s)
                        bank = pb[2 + m % 2][:, :]
                        ac.inc(s.activation(y1b[:, m, :], bank, AF.Gelu))

            for m in range(DC):
                with nc.Block() as blk:
                    @blk.tensor
                    def _(t, m=m):
                        ac.wait(t)
                        ve.wait(t)
                        bank = pb[4 + m % 2][:, :]
                        for k in range(DC):
                            t.matmul(bank, w2s[:, k, 128 * m:128 * (m + 1)], y1b[:, k, :],
                                     start=(k == 0), stop=False)
                        pe.inc(t.matmul(bank, w2brow[:, 128 * m:128 * (m + 1)],
                                        self.ones512[:], start=False, stop=True))

                    @blk.vector
                    def _(v, m=m):
                        pe.wait(v)
                        bank = pb[4 + m % 2][:, :]
                        self.vs(v, v.tensor_add(yb[:, m, :], bank, outb[:, m, :]))

            for tt in range(4):
                gi = kc * 4 + tt
                with nc.Block() as blk:
                    @blk.gpsimd
                    def _(g_, kc=kc, tt=tt, gi=gi):
                        if gi > 0:
                            g_.wait_ge(ve.sem, acnt[gi - 1])
                            g_.wait_ge(sgd.sem, gcnt[gi - 1])
                            for e_ in range(NE):
                                sgd.inc(g_.dma_gather(
                                    xadd[:, e_:e_ + 1, :], self.moebuf[e_],
                                    self.inv16[:, e_, 8 * gi:8 * gi + 8],
                                    128, r128["r"], D, elem_step=D), 16)
                            gcnt[gi] = sgd.n

                    @blk.scalar
                    def _(s, kc=kc, tt=tt, gi=gi):
                        ve.wait(s)
                        lda.inc(s.dma_start(
                            xt2[:, gi % 2, :],
                            din["x_full"][self.row0 + gi * 128:
                                          self.row0 + (gi + 1) * 128, :]), 16)

                    @blk.tensor
                    def _(t, kc=kc, tt=tt):
                        ve.wait(t)
                        for k in range(DC):
                            ins = t.transpose(psb[:, 128 * k:128 * (k + 1)],
                                              yb[:, k, 128 * tt:128 * (tt + 1)],
                                              self.identb[:])
                            if k == DC - 1:
                                pe.inc(ins)

                    @blk.vector
                    def _(v, kc=kc, par=par, tt=tt):
                        pe.wait(v)
                        self.vs(v, v.bn_stats(self.mv[:, 36, 0:6], psb[:, 0:512]))
                        self.vs(v, v.bn_stats(self.mv[:, 36, 6:12], psb[:, 512:1024]))
                        self.vs(v, v.bn_aggr(self.mv[:, 36, 12:14], self.mv[:, 36, 0:12]))

                    @blk.scalar
                    def _(s, tt=tt):
                        ve.wait(s)
                        ac.inc(s.activation(self.mv[:, 36, 14:15], self.mv[:, 36, 13:14],
                                            AF.Sqrt, bias=self.epsc[:, 0:1]))

                    @blk.vector
                    def _(v, kc=kc, par=par, tt=tt, gi=gi):
                        ac.wait(v)
                        if gi >= 2:
                            self.stx[gi % 2].wait(v)
                        v.wait_ge(sgd.sem, gcnt[gi])
                        self.vs(v, v.reciprocal(self.mv[:, 36, 15:16], self.mv[:, 36, 14:15]))
                        self.vs(v, v.tensor_scalar(self.s1[:, 0:1024], psb[:],
                                                   self.mv[:, 36, 12:13], self.mv[:, 36, 15:16],
                                                   OP.subtract, OP.mult))
                        self.vs(v, v.tensor_tensor(self.s1[:, 0:1024], self.s1[:, 0:1024],
                                                   gbc[:], OP.mult))
                        self.vs(v, v.tensor_add(self.s1[:, 0:1024], self.s1[:, 0:1024], bbc[:]))
                        lda.wait(v)
                        self.vs(v, v.tensor_add(x1t[:, gi % 2, :],
                                                self.s1[:, 0:1024], xt2[:, gi % 2, :]))
                        for e_ in range(NE):
                            self.vs(v, v.tensor_add(x1t[:, gi % 2, :], x1t[:, gi % 2, :],
                                                    xadd[:, e_, :]))
                        acnt[gi] = ve.n

                    @blk.sync
                    def _(sy, kc=kc, tt=tt, tok0=tok0):
                        ve.wait(sy)
                        self.stx[(kc * 4 + tt) % 2].inc(sy.dma_start(
                            self.x1_hbm[tok0 + tt * 128:tok0 + (tt + 1) * 128, :],
                            x1t[:, (kc * 4 + tt) % 2, :]), 16)
            self.bar()
        with nc.Block() as blk:
            @blk.sync
            def _(sy):
                self.stx[0].wait(sy)
                self.stx[1].wait(sy)
        self.bar()
        R.close()

    def p4_moe(self):
        nc, din = self.nc, self.din
        lsw, pe, ve, ac, gp, sgd, wl = (self.lsw, self.pe, self.ve, self.ac,
                                        self.gp, self.sgd, self.wl)
        R = ExitStack()
        r = R.enter_context
        xgA = r(nc.sbuf_tensor("p4_xgA", [128, DC, 640], BF16))
        xgB = r(nc.sbuf_tensor("p4_xgB", [128, DC, 512], BF16))
        hid = r(nc.sbuf_tensor("p4_hid", [128, FC, CAP], BF16))
        wbuf = r(nc.sbuf_tensor("p4_wbuf", [128, 4, 8, 1024], BF16))
        osrc = r(nc.sbuf_tensor("p4_osrc", [128, 4, 1024], BF16))
        zrow = r(nc.sbuf_tensor("p4_zrow", [1, D], BF16))
        cgA = r(nc.sbuf_tensor("p4_cgA", [1, 1], U32))
        cgB = r(nc.sbuf_tensor("p4_cgB", [1, 1], U32))
        b1e = r(nc.sbuf_tensor("p4_b1e", [1, FF], BF16))
        ones512b = r(nc.sbuf_tensor("p4_ones512b", [1, 512], BF16))
        b2e = r(nc.sbuf_tensor("p4_b2e", [1, D], BF16))
        pb = [r(nc.psum_tensor(f"p4_pb{i}", [128, 512], F32)) for i in range(8)]

        regs = {}
        with nc.Block() as blk:
            @blk.vector
            def _(v):
                self.vs(v, v.memset(xgA[:], 0.0))
                self.vs(v, v.memset(xgB[:], 0.0))
                self.vs(v, v.memset(ones512b[:], 1.0))
                self.vs(v, v.memset(zrow[:], 0.0))
                self.vs(v, v.memset(cgA[:], 640))
                self.vs(v, v.memset(cgB[:], 512))

            @blk.gpsimd
            def _(g):
                g.load_library(LIB_MLP)
                ve.wait(g)
                regs["cA"] = g.alloc_register("r_cA")
                regs["cB"] = g.alloc_register("r_cB")
                g.load(regs["cA"], cgA[0:1, 0:1])
                g.load(regs["cB"], cgB[0:1, 0:1])

            @blk.sync
            def _(sy):
                ve.wait(sy)
                for e_ in range(NE):
                    self.st[1].inc(sy.dma_start(self.moebuf[e_, CAP:CAP + 1, :],
                                                zrow[:]), 16)
        self.bar()

        for ee in range(NE):
            parb = ee % 2
            with nc.Block() as blk:
                @blk.gpsimd
                def _(g, ee=ee):
                    ve.wait(g)
                    ac.wait(g)
                    pe.wait(g)
                    gp.wait(g)
                    lsw.inc(g.dma_start(b1e[:], din["b1"][ee:ee + 1, :]), 16)
                    lsw.inc(g.dma_start(b2e[:], din["b2"][ee:ee + 1, :]), 16)
                    lsw.wait(g)
                    gp.inc(g.nop())
                    g.load_library(LIB_MLP)
                    sgd.inc(g.dma_gather(xgA[:], self.xn_hbm[:], self.lists[:, ee, 0:40],
                                         640, regs["cA"], D, elem_step=D,
                                         transpose=True), 16)
                    sgd.inc(g.dma_gather(xgB[:], self.xn_hbm[:], self.lists[:, ee, 40:72],
                                         512, regs["cB"], D, elem_step=D,
                                         transpose=True), 16)

                @blk.sync
                def _(sy, ee=ee):
                    pe.wait(sy)
                    if ee == 0:
                        self.wcast.wait(sy)
                    for qm in range(4):
                        for k in range(DC):
                            wl[qm].inc(sy.dma_start(
                                wbuf[:, qm, k, :],
                                self.w1bf[ee, k * 128:(k + 1) * 128,
                                          qm * 1024:(qm + 1) * 1024]), 16)

            for m in range(FC):
                qm = m // 8
                with nc.Block() as blk:
                    @blk.tensor
                    def _(t, ee=ee, m=m, qm=qm, parb=parb):
                        wl[qm].wait(t)
                        if m == 0:
                            sgd.wait(t)
                            lsw.wait(t)
                            ve.wait(t)
                        ac.wait(t)
                        stts = ((xgA, 0, 512, 0), (xgA, 512, 128, 512),
                                (xgB, 0, 512, 640))
                        for stt, (xgt, off, w, hb) in enumerate(stts):
                            bsl = pb[stt + 3 * (m % 2)][:, 0:w]
                            for k in range(DC):
                                t.matmul(bsl, wbuf[:, qm, k, 128 * (m % 8):128 * (m % 8 + 1)],
                                         xgt[:, k, off:off + w], start=(k == 0), stop=False)
                            ins = t.matmul(bsl, b1e[:, 128 * m:128 * (m + 1)],
                                           ones512b[:, 0:w], start=False, stop=True)
                            if stt == 2:
                                pe.inc(ins)

                    @blk.scalar
                    def _(s, m=m):
                        pe.wait(s)
                        stts = ((0, 512, 0), (512, 128, 512), (0, 512, 640))
                        for stt, (off, w, hb) in enumerate(stts):
                            ins = s.activation(hid[:, m, hb:hb + w],
                                               pb[stt + 3 * (m % 2)][:, 0:w], AF.Gelu)
                            if stt == 2:
                                ac.inc(ins)

            for scg in range(3):
                nsc = 4 if scg < 2 else 1
                with nc.Block() as blk:
                    if scg == 0:
                        @blk.sync
                        def _(sy, ee=ee):
                            pe.wait(sy)
                            for kq in range(4):
                                for k in range(DC):
                                    wl[kq].inc(sy.dma_start(
                                        wbuf[:, kq, k, :],
                                        self.w2bf[ee,
                                                  kq * 1024 + k * 128:
                                                  kq * 1024 + (k + 1) * 128,
                                                  :]), 16)

                    @blk.tensor
                    def _(t, ee=ee, scg=scg, nsc=nsc, parb=parb):
                        ac.wait(t)
                        ve.wait(t)
                        for kq in range(4):
                            wl[kq].wait(t)
                            for k in range(DC):
                                kabs = kq * 8 + k
                                for sc in range(nsc):
                                    scabs = scg * 4 + sc
                                    for n in range(2):
                                        nsl = slice(n * 512, n * 512 + 512)
                                        t.matmul(pb[2 * sc + n][:, :],
                                                 hid[:, kabs, 128 * scabs:128 * (scabs + 1)],
                                                 wbuf[:, kq, k, nsl],
                                                 start=(kabs == 0), stop=False)
                        for sc in range(nsc):
                            for n in range(2):
                                nsl = slice(n * 512, n * 512 + 512)
                                ins = t.matmul(pb[2 * sc + n][:, :], self.onesrb[:],
                                               b2e[:, nsl],
                                               start=False, stop=True)
                            pe.inc(ins)

                    @blk.scalar
                    def _(s, ee=ee, scg=scg, nsc=nsc):
                        pe.wait(s)
                        self.st[1].wait(s)
                        for sc in range(nsc):
                            for n in range(2):
                                nsl = slice(n * 512, n * 512 + 512)
                                ins = s.activation(osrc[:, sc, nsl], pb[2 * sc + n][:, :],
                                                   AF.Copy,
                                                   scale=self.gv9[:, ee, scg * 4 + sc:scg * 4 + sc + 1])
                                ac.inc(ins)

                    @blk.sync
                    def _(sy, ee=ee, scg=scg, nsc=nsc):
                        ac.wait(sy)
                        for sc in range(nsc):
                            scabs = scg * 4 + sc
                            self.st[1].inc(sy.dma_start(
                                self.moebuf[ee, 128 * scabs:128 * (scabs + 1), :],
                                osrc[:, sc, :]), 16)
            self.bar()
        with nc.Block() as blk:
            @blk.gpsimd
            def _(g):
                sgd.wait(g)

            @blk.sync
            def _(sy):
                self.st[1].wait(sy)
        self.bar()
        R.close()

    def p6_layer1(self):
        nc, din = self.nc, self.din
        ld, lda, lsw, pe, ve, ac, st, trl = (self.ld, self.lda, self.lsw, self.pe,
                                             self.ve, self.ac, self.st, self.trl)
        R = ExitStack()
        r = R.enter_context
        caus = r(nc.sbuf_tensor("p6_caus", [128, 128], BF16))
        wat = r(nc.sbuf_tensor("p6_wat", [128, DC, 3 * D], BF16))
        wpr = r(nc.sbuf_tensor("p6_wpr", [128, DC, D], BF16))
        sgnT = r(nc.sbuf_tensor("p6_sgnT", [128, DC, D], BF16))
        sgntm = r(nc.sbuf_tensor("p6_sgntm", [128, D], BF16))
        xnsl = r(nc.sbuf_tensor("p6_xnsl", [128, DC, 512], BF16))
        qfm = r(nc.sbuf_tensor("p6_qfm", [128, DC, 512], BF16))
        kfm = r(nc.sbuf_tensor("p6_kfm", [128, DC, 512], BF16))
        vfm = r(nc.sbuf_tensor("p6_vfm", [128, DC, 512], BF16))
        ktvt = r(nc.sbuf_tensor("p6_ktvt", [128, 2, 128], BF16))
        atb = r(nc.sbuf_tensor("p6_atb", [128, 128], BF16))
        nsb = r(nc.sbuf_tensor("p6_nsb", [128, 128], BF16))
        S_all = r(nc.sbuf_tensor("p6_Sall", [128, 8, 64], BF16))
        z_all = r(nc.sbuf_tensor("p6_zall", [128, 8, 1], BF16))
        S32 = r(nc.sbuf_tensor("p6_S32", [128, 512], F32))
        z32 = r(nc.sbuf_tensor("p6_z32", [128, 8], F32))
        rden = r(nc.sbuf_tensor("p6_rden", [128, 16], F32))
        ytm = r(nc.sbuf_tensor("p6_ytm", [128, 2, 1024], BF16))
        yfm = r(nc.sbuf_tensor("p6_yfm", [128, DC, 512], BF16))
        l1s = r(nc.sbuf_tensor("p6_l1s", [128, DC, 512], F32))
        x1t = r(nc.sbuf_tensor("p6_x1t", [128, 2, 1024], F32))
        otm = r(nc.sbuf_tensor("p6_otm", [128, 2, 1024], F32))
        pb = [r(nc.psum_tensor(f"p6_pb{i}", [128, 512], F32)) for i in range(6)]
        psb = r(nc.psum_tensor("p6_psb", [128, 1024], BF16))

        with nc.Block() as blk:
            @blk.sync
            def _(sy):
                ld.inc(sy.dma_start(caus[:], self.CST["caus_b"]), 16)

            @blk.gpsimd
            def _(g):
                g.load_library(LIB_STD)
                for k in range(DC):
                    lsw.inc(g.dma_start(wat[:, k, :], din["wattn"][k * 128:(k + 1) * 128, :]), 16)
                    lsw.inc(g.dma_start(wpr[:, k, :], din["wproj"][k * 128:(k + 1) * 128, :]), 16)

        for rt in range(DC):
            with nc.Block() as blk:
                @blk.sync
                def _(sy, rt=rt):
                    ac.wait(sy)
                    ld.inc(sy.dma_start(self.s1[:, 0:1024],
                                        din["L1w"][rt * 128:(rt + 1) * 128, :]), 16)

                @blk.scalar
                def _(s, rt=rt):
                    ld.wait(s)
                    ac.inc(s.activation(sgntm[:], self.s1[:, 0:1024], AF.Sign))

                @blk.tensor
                def _(t, rt=rt):
                    ac.wait(t)
                    ve.wait(t)
                    for k in range(DC):
                        ins = t.transpose(psb[:, 128 * k:128 * (k + 1)],
                                          sgntm[:, 128 * k:128 * (k + 1)], self.identb[:])
                        if k == DC - 1:
                            pe.inc(ins)

                @blk.vector
                def _(v, rt=rt):
                    pe.wait(v)
                    for k in range(DC):
                        self.ve.inc(v.tensor_copy(sgnT[:, k, 128 * rt:128 * (rt + 1)],
                                                  psb[:, 128 * k:128 * (k + 1)]))
                    v.wait_ge(ve.sem, ve.n)
        self.bar()

        NSUB = CH // 128
        for kc in range(NCHUNK):
            tok0 = kc * CH
            par = kc % 2
            with nc.Block() as blk:
                @blk.sync
                def _(sy, kc=kc, par=par, tok0=tok0):
                    pe.wait(sy)
                    for k in range(DC):
                        trl[0].inc(sy.dma_start(
                            xnsl[:, k, :],
                            self.xn2_hbm[tok0:tok0 + CH, k * 128:(k + 1) * 128],
                            transpose=True), 16)

            for m in range(24):
                with nc.Block() as blk:
                    @blk.tensor
                    def _(t, kc=kc, par=par, m=m):
                        trl[0].wait(t)
                        lsw.wait(t)
                        ve.wait(t)
                        ac.wait(t)
                        bank = pb[m % 2][:, :]
                        for k in range(DC):
                            ins = t.matmul(bank, wat[:, k, 128 * m:128 * (m + 1)],
                                           xnsl[:, k, :], start=(k == 0),
                                           stop=(k == DC - 1))
                        pe.inc(ins)

                    if m < 16:
                        @blk.vector
                        def _(v, m=m):
                            pe.wait(v)
                            bank = pb[m % 2][:, :]
                            self.vs(v, v.tensor_scalar(self.s1[:, 0:512], bank, 0.0,
                                                       None, OP.min))

                        @blk.scalar
                        def _(s, m=m):
                            ve.wait(s)
                            ac.inc(s.activation(self.s2[:, 0:512], self.s1[:, 0:512], AF.Exp))

                        @blk.vector
                        def _(v, m=m):
                            ac.wait(v)
                            bank = pb[m % 2][:, :]
                            dst = qfm if m < 8 else kfm
                            self.vs(v, v.tensor_scalar(self.s1[:, 0:512], bank, 0.0,
                                                       None, OP.max))
                            self.vs(v, v.tensor_add(dst[:, m % 8, :], self.s1[:, 0:512],
                                                    self.s2[:, 0:512]))
                    else:
                        @blk.scalar
                        def _(s, m=m):
                            pe.wait(s)
                            bank = pb[m % 2][:, :]
                            ac.inc(s.activation(vfm[:, m % 8, :], bank, AF.Copy))

            for c in range(NSUB):
                cglob = (kc % (NCHUNK // 2)) * NSUB + c
                csl = slice(128 * c, 128 * (c + 1))
                with nc.Block() as blk:
                    @blk.vector
                    def _(v, kc=kc, c=c, cglob=cglob):
                        ac.wait(v)
                        pe.wait(v)
                        if cglob > 0:
                            self.vs(v, v.tensor_add(S32[:], S32[:], pb[4][:, :]))
                            self.vs(v, v.tensor_add(z32[:], z32[:], pb[5][:, 0:8]))
                            self.vs(v, v.tensor_copy(
                                S_all.ap().rearrange("p a b -> p (a b)"), S32[:]))
                            self.vs(v, v.tensor_copy(
                                z_all.ap().rearrange("p a b -> p (a b)"), z32[:]))
                        else:
                            self.vs(v, v.memset(S32[:], 0.0))
                            self.vs(v, v.memset(z32[:], 0.0))
                            self.vs(v, v.memset(S_all[:], 0.0))
                            self.vs(v, v.memset(z_all[:], 0.0))

                for h in range(NHEAD):
                    hp = h // 2
                    po = 64 * (h % 2)
                    with nc.Block() as blk:
                        @blk.tensor
                        def _(t, kc=kc, c=c, h=h, hp=hp, po=po, cglob=cglob, csl=csl):
                            ve.wait(t)
                            t.transpose(psb[:, 0:64], kfm[po:po + 64, hp, csl], self.identb[po:po + 64, po:po + 64])
                            pe.inc(t.transpose(psb[:, 64:128], vfm[po:po + 64, hp, csl],
                                               self.identb[po:po + 64, po:po + 64]))

                        @blk.vector
                        def _(v, h=h):
                            pe.wait(v)
                            self.vs(v, v.tensor_copy(ktvt[:, 0, 0:64], psb[:, 0:64]))
                            self.vs(v, v.tensor_copy(ktvt[:, 1, 0:64], psb[:, 64:128]))

                        @blk.tensor
                        def _(t, kc=kc, c=c, h=h, hp=hp, po=po, cglob=cglob, csl=csl):
                            ve.wait(t)
                            pe.inc(t.matmul(pb[2][:, 0:128], kfm[po:po + 64, hp, csl],
                                            qfm[po:po + 64, hp, csl], start=True, stop=True))

                        @blk.vector
                        def _(v, h=h):
                            pe.wait(v)
                            self.vs(v, v.tensor_tensor(atb[:], pb[2][:, 0:128],
                                                       caus[:], OP.mult))

                        @blk.tensor
                        def _(t, kc=kc, c=c, h=h, hp=hp, po=po, cglob=cglob, csl=csl):
                            ve.wait(t)
                            nbank = pb[3][0:64, 0:128]
                            t.matmul(nbank, ktvt[:, 1, 0:64], atb[:], start=True, stop=False)
                            t.matmul(nbank, S_all[po:po + 64, hp, :],
                                     qfm[po:po + 64, hp, csl], start=False, stop=True)
                            dbank = pb[2][:, 256 + h:256 + h + 1]
                            t.matmul(dbank, atb[:], self.onesc[:], start=True, stop=False)
                            pe.inc(t.matmul(dbank, qfm[po:po + 64, hp, csl],
                                            z_all[po:po + 64, hp, :], start=False, stop=True))
                            sbank = pb[4][po:po + 64, 64 * hp:64 * hp + 64]
                            t.matmul(sbank, ktvt[:, 0, 0:64], ktvt[:, 1, 0:64],
                                     start=True, stop=True)
                            zbank = pb[5][po:po + 64, hp:hp + 1]
                            pe.inc(t.matmul(zbank, ktvt[:, 0, 0:64], self.onesc[:],
                                            start=True, stop=True))

                        @blk.vector
                        def _(v, h=h):
                            pe.wait(v)
                            self.vs(v, v.tensor_copy(nsb[0:64, :], pb[3][0:64, 0:128]))

                        @blk.tensor
                        def _(t, h=h):
                            ve.wait(t)
                            pe.inc(t.transpose(psb[:, 128:192], nsb[0:64, :], self.identb[0:64, 0:64]))

                        @blk.vector
                        def _(v, kc=kc, c=c, h=h):
                            pe.wait(v)
                            self.vs(v, v.tensor_copy(self.s1[:, 64 * h:64 * h + 64],
                                                     psb[:, 128:192]))

                with nc.Block() as blk:
                    @blk.vector
                    def _(v, kc=kc, c=c):
                        pe.wait(v)
                        st[2 + c % 2].wait(v)
                        self.vs(v, v.tensor_scalar(rden[:, 0:16], pb[2][:, 256:272],
                                                   float(EPS_ATT), None, OP.add))
                        self.vs(v, v.reciprocal(rden[:, 0:16], rden[:, 0:16]))
                        for h in range(NHEAD):
                            self.vs(v, v.tensor_scalar(ytm[:, c % 2, 64 * h:64 * h + 64],
                                                       self.s1[:, 64 * h:64 * h + 64],
                                                       rden[:, h:h + 1], None, OP.mult))

                    @blk.sync
                    def _(sy, kc=kc, c=c, tok0=tok0):
                        ve.wait(sy)
                        st[2 + c % 2].inc(sy.dma_start(
                            self.yat_hbm[tok0 + 128 * c:tok0 + 128 * (c + 1), :],
                            ytm[:, c % 2, :]), 16)
            self.bar()

        with nc.Block() as blk:
            @blk.sync
            def _(sy):
                st[2].wait(sy)
                st[3].wait(sy)
        self.bar()

        for kc in range(NCHUNK):
            tok0 = kc * CH
            par = kc % 2
            with nc.Block() as blk:
                @blk.sync
                def _(sy, kc=kc, par=par, tok0=tok0):
                    pe.wait(sy)
                    for k in range(DC):
                        trl[0].inc(sy.dma_start(
                            yfm[:, k, :],
                            self.yat_hbm[tok0:tok0 + CH, k * 128:(k + 1) * 128],
                            transpose=True), 16)
                        trl[0].inc(sy.dma_start(
                            xnsl[:, k, :],
                            self.xn2_hbm[tok0:tok0 + CH, k * 128:(k + 1) * 128],
                            transpose=True), 16)

            for m in range(DC):
                with nc.Block() as blk:
                    @blk.tensor
                    def _(t, kc=kc, par=par, m=m):
                        trl[0].wait(t)
                        ve.wait(t)
                        bank = pb[m % 2][:, :]
                        for k in range(DC):
                            t.matmul(bank, wpr[:, k, 128 * m:128 * (m + 1)],
                                     yfm[:, k, :], start=(k == 0), stop=False)
                        for k in range(DC):
                            ins = t.matmul(bank, sgnT[:, k, 128 * m:128 * (m + 1)],
                                           xnsl[:, k, :], start=False,
                                           stop=(k == DC - 1))
                        pe.inc(ins)

                    @blk.vector
                    def _(v, m=m):
                        pe.wait(v)
                        bank = pb[m % 2][:, :]
                        self.vs(v, v.tensor_copy(l1s[:, m, :], bank))

            for tt in range(4):
                with nc.Block() as blk:
                    @blk.scalar
                    def _(s, kc=kc, tt=tt, tok0=tok0):
                        ve.wait(s)
                        lda.inc(s.dma_start(
                            x1t[:, tt % 2, :],
                            self.x1_hbm[tok0 + tt * 128:tok0 + (tt + 1) * 128, :]), 16)

                    @blk.tensor
                    def _(t, kc=kc, tt=tt):
                        ve.wait(t)
                        for k in range(DC):
                            pbk = pb[4] if k < 4 else pb[5]
                            ins = t.transpose(pbk[:, 128 * (k % 4):128 * (k % 4 + 1)],
                                              l1s[:, k, 128 * tt:128 * (tt + 1)],
                                              self.ident[:])
                            if k == DC - 1:
                                pe.inc(ins)

                    @blk.vector
                    def _(v, kc=kc, tt=tt):
                        pe.wait(v)
                        lda.wait(v)
                        if kc * 4 + tt >= 2:
                            self.stx[(kc * 4 + tt) % 2].wait(v)
                        self.vs(v, v.tensor_add(otm[:, tt % 2, 0:512],
                                                pb[4][:, :], x1t[:, tt % 2, 0:512]))
                        self.vs(v, v.tensor_add(otm[:, tt % 2, 512:1024],
                                                pb[5][:, :], x1t[:, tt % 2, 512:1024]))

                    @blk.sync
                    def _(sy, kc=kc, tt=tt, tok0=tok0):
                        ve.wait(sy)
                        self.stx[(kc * 4 + tt) % 2].inc(sy.dma_start(
                            self.out_hbm[tok0 + tt * 128:tok0 + (tt + 1) * 128, :],
                            otm[:, tt % 2, :]), 16)
            self.bar()
        with nc.Block() as blk:
            @blk.sync
            def _(sy):
                self.stx[0].wait(sy)
                self.stx[1].wait(sy)
        self.bar()
        R.close()

    def build(self):
        nc = self.nc
        with nc.named_scope("p0"):
            self.p0_loads()
        with nc.named_scope("p1_ln"):
            self.p_ln(True, self.xn_hbm)
        with nc.named_scope("p2_scores"):
            self.p2_scores()
        with nc.named_scope("p4_moe"):
            self.p4_moe()
        with nc.named_scope("p3a_scan"):
            self.p3a_scan()
        with nc.named_scope("p3b_tail"):
            self.p3b_tail()
        with nc.named_scope("p5_ln2"):
            self.p_ln(False, self.xn2_hbm)
        with nc.named_scope("p6_layer1"):
            self.p6_layer1()
        self.S.close()
        return self.nc


def build_program():
    kb = KB()
    nc = kb.build()
    return nc


def kernel(**inputs):
    x = np.ascontiguousarray(inputs["L0_attn_x"], dtype=np.float32).reshape(NTOK_G, D)
    base = {
        "x_full": x,
        "Win": np.ascontiguousarray(inputs["L0_attn_W_in"], np.float32),
        "bin_": np.ascontiguousarray(inputs["L0_attn_b_in"], np.float32),
        "plog": np.ascontiguousarray(inputs["L0_attn_params_log"], np.float32),
        "Wmid": np.ascontiguousarray(inputs["L0_attn_W_mid"], np.float32),
        "bmid": np.ascontiguousarray(inputs["L0_attn_b_mid"], np.float32),
        "w1w": np.ascontiguousarray(inputs["L0_attn_w1_w"], np.float32),
        "w1b": np.ascontiguousarray(inputs["L0_attn_w1_b"], np.float32).reshape(1, D),
        "w2w": np.ascontiguousarray(inputs["L0_attn_w2_w"], np.float32),
        "w2b": np.ascontiguousarray(inputs["L0_attn_w2_b"], np.float32).reshape(1, D),
        "lng": np.ascontiguousarray(inputs["L0_attn_ln_g"], np.float32).reshape(1, D),
        "lnb": np.ascontiguousarray(inputs["L0_attn_ln_b"], np.float32).reshape(1, D),
        "Wg": np.ascontiguousarray(inputs["L0_ffn_Wg"], np.float32),
        "W1": np.ascontiguousarray(inputs["L0_ffn_W1"], np.float32),
        "b1": np.ascontiguousarray(inputs["L0_ffn_b1"], np.float32),
        "W2": np.ascontiguousarray(inputs["L0_ffn_W2"], np.float32),
        "b2": np.ascontiguousarray(inputs["L0_ffn_b2"], np.float32),
        "wattn": np.ascontiguousarray(inputs["L1_attn_w_attn"], np.float32),
        "wproj": np.ascontiguousarray(inputs["L1_attn_w_proj"], np.float32),
        "L1w": np.ascontiguousarray(inputs["L1_ffn_weight"], np.float32),
    }
    nc = build_program()
    lower_extended_insts(nc)
    in_maps = []
    for i in range(NCORES):
        m = dict(base)
        m["x_full"] = np.roll(x, -NTOK * i, axis=0) if i else x
        in_maps.append(m)
    res = run_bass_kernel_spmd(nc, in_maps, core_ids=list(range(NCORES)))
    out = np.concatenate([res.results[i]["out"] for i in range(NCORES)], axis=0)
    return out.reshape(B, T, D)
